# version 24
# baseline (speedup 1.0000x reference)
import math
import os
import numpy as np
import ml_dtypes
import concourse.bass as bass
import concourse.mybir as mybir
from concourse.bass_utils import run_bass_kernel_spmd
from concourse.bass import IndirectOffsetOnAxis
from contextlib import ExitStack

F32 = mybir.dt.float32
BF16 = mybir.dt.bfloat16
I32 = mybir.dt.int32
AF = mybir.ActivationFunctionType
ALU = mybir.AluOpType
AX = mybir.AxisListType

NC = 8
B, S, D = 4, 2048, 4096
T = B * S
TS = T // NC
H, HD = 32, 128
HPC = H // NC
W3 = 3 * HPC * HD
INTER, RANK, SPB = 14336, 16, 256
NBLK = INTER // SPB
BPC = NBLK // NC
SLICE = BPC * SPB
SCALE = HD ** -0.5
EPS = 1e-5
BIGIDX = float(2 ** 23)
NQC = S // 128
NTC = T // 128
NT_SL = TS // 128
STAGE = os.environ.get("MK_STAGE", "full")

ENGS = ("sy", "pe", "dv", "ac", "gp")


class Prog:
    def __init__(self, nc, st):
        self.nc = nc
        self.events = {e: [] for e in ENGS}
        self.cnt = {e: 0 for e in ENGS}
        self.sems = {e: st.enter_context(nc.semaphore(f"sem_{e}")) for e in ENGS}
        self.ncc = 0
        self.st = st

    def xsem(self, name):
        if name not in self.sems:
            self.sems[name] = self.st.enter_context(
                self.nc.semaphore(f"sem_{name}"))
            self.cnt[name] = 0
        return name

    def ev(self, eng, emit, deps=(), tag=""):
        self.cnt[eng] += 1
        tok = (eng, self.cnt[eng], False)
        self.events[eng].append((list(deps), emit, 1, tag, None))
        return tok

    def dma(self, eng, emit, deps=(), tag="", sem=None):
        ch = eng if sem is None else self.xsem(sem)
        self.cnt[ch] += 16
        tok = (ch, self.cnt[ch], True)
        self.events[eng].append((list(deps), emit, 16, tag, ch))
        return tok

    def cc(self, emit, deps=(), tag=""):
        sem = self.st.enter_context(self.nc.semaphore(f"ccs{self.ncc}"))
        self.ncc += 1

        def emit2(g):
            emit(g).then_inc(sem, 1)
            return None
        self.events["gp"].append((list(deps), emit2, 0, tag, None))
        return ("CC", sem, True)

    def run(self, blk):
        def body(eng_name):
            def _(eng):
                waited = {}
                mysem = self.sems[eng_name]
                for deps, emit, inc, tag, ch in self.events[eng_name]:
                    for d in deps:
                        if d is None:
                            continue
                        if d[0] == "CC":
                            key = id(d[1])
                            if waited.get(key, 0) < 1:
                                eng.wait_ge(d[1], 1)
                                waited[key] = 1
                            continue
                        de, dv_, asyn = d
                        if waited.get(de, 0) < dv_:
                            eng.wait_ge(self.sems[de], dv_)
                            waited[de] = dv_
                    ins = emit(eng)
                    if inc:
                        tsem = mysem if ch is None else self.sems[ch]
                        if ins is None:
                            eng.sem_inc(tsem, inc)
                        else:
                            ins.then_inc(tsem, inc)
            return _
        blk.sync(body("sy"))
        blk.tensor(body("pe"))
        blk.vector(body("dv"))
        blk.scalar(body("ac"))
        blk.gpsimd(body("gp"))


def get_alibi_slopes(n):
    c = 2 ** math.floor(math.log2(n))
    base = 2 ** (-(2 ** -(math.log2(c) - 3)))
    slopes = np.power(base, np.arange(1, 1 + c, dtype=np.float32)).astype(np.float32)
    if c != n:
        eb = 2 ** (-(2 ** -(math.log2(2 * c) - 3)))
        num = min(c, n - c)
        ep = np.arange(1, 1 + 2 * num, 2, dtype=np.float32)
        slopes = np.concatenate([slopes, np.power(eb, ep).astype(np.float32)])
    return slopes


def build():
    nc = bass.Bass("TRN2", target_bir_lowering=False)
    ap = lambda t: t.ap()

    x_slab = ap(nc.dram_tensor("x_slab", [TS, D], F32, kind="ExternalInput"))
    wq_sl = ap(nc.dram_tensor("wq_sl", [D, W3], F32, kind="ExternalInput"))
    bq_sl = ap(nc.dram_tensor("bq_sl", [1, W3], F32, kind="ExternalInput"))
    wo_sl = ap(nc.dram_tensor("wo_sl", [HPC * HD, D], F32, kind="ExternalInput"))
    bout_bc = ap(nc.dram_tensor("bout_bc", [128, D], F32, kind="ExternalInput"))
    c1t = ap(nc.dram_tensor("c1t", [D, RANK], F32, kind="ExternalInput"))
    c2t_sl = ap(nc.dram_tensor("c2t_sl", [RANK, SLICE], F32, kind="ExternalInput"))
    w1_sl = ap(nc.dram_tensor("w1_sl", [SLICE, D], F32, kind="ExternalInput"))
    w2_sl = ap(nc.dram_tensor("w2_sl", [SLICE, D], F32, kind="ExternalInput"))
    alibi_r = ap(nc.dram_tensor("alibi_r", [HPC, 2, S], BF16, kind="ExternalInput"))
    tri_c = ap(nc.dram_tensor("tri_c", [128, 128], F32, kind="ExternalInput"))
    ident_c = ap(nc.dram_tensor("ident_c", [128, 128], BF16, kind="ExternalInput"))
    onesb_c = ap(nc.dram_tensor("onesb_c", [2, 128], BF16, kind="ExternalInput"))

    out_slab = ap(nc.dram_tensor("out_slab", [TS, D], F32, kind="ExternalOutput"))

    dt_ = lambda n_, s_, d_: ap(nc.dram_tensor(n_, s_, d_, kind="Internal"))
    wqb_d = dt_("wqb_d", [D, W3], BF16)
    c1tb_d = dt_("c1tb_d", [D, RANK], BF16)
    w1b_d = dt_("w1b_d", [SLICE, D], BF16)
    w2b_d = dt_("w2b_d", [SLICE, D], BF16)
    xin_sl_d = dt_("xin_sl_d", [TS, D], BF16)
    xin_fu0_d = dt_("xin_fu0_d", [T // 2, D], BF16)
    xin_fu1_d = dt_("xin_fu1_d", [T // 2, D], BF16)
    ctxT_d = dt_("ctxT_d", [HPC * HD, T], BF16)
    att_pt0_d = dt_("att_pt0_d", [T // 2, D], BF16)
    guard0_d = dt_("guard0_d", [T // 2, D], BF16)
    att_pt1_d = dt_("att_pt1_d", [T // 2, D], BF16)
    guard1_d = dt_("guard1_d", [T // 2, D], BF16)
    att_rs0_d = dt_("att_rs0_d", [TS // 2, D], BF16)
    att_rs1_d = dt_("att_rs1_d", [TS // 2, D], BF16)
    xp_d = dt_("xp_d", [TS, D], F32)
    xf_sl_d = dt_("xf_sl_d", [TS, D], BF16)
    xf_fu0_d = dt_("xf_fu0_d", [T // 2, D], BF16)
    xf_fu1_d = dt_("xf_fu1_d", [T // 2, D], BF16)
    r_sl0_d = dt_("r_sl0_d", [RANK, TS // 2], BF16)
    r_sl1_d = dt_("r_sl1_d", [RANK, TS // 2], BF16)
    r_st0_d = dt_("r_st0_d", [NC * RANK, TS // 2], BF16)
    r_st1_d = dt_("r_st1_d", [NC * RANK, TS // 2], BF16)
    ffn_pt0_d = dt_("ffn_pt0_d", [T // 2, D], BF16)
    guard2_d = dt_("guard2_d", [T // 2, D], BF16)
    ffn_pt1_d = dt_("ffn_pt1_d", [T // 2, D], BF16)
    guard3_d = dt_("guard3_d", [T // 2, D], BF16)
    ffn_rs0_d = dt_("ffn_rs0_d", [TS // 2, D], BF16)
    ffn_rs1_d = dt_("ffn_rs1_d", [TS // 2, D], BF16)
    dbg_d = dt_("dbg_d", [TS, D], F32)

    st = ExitStack()
    with st:
        P = Prog(nc, st)
        sb = lambda n_, s_, d_=F32: st.enter_context(nc.sbuf_tensor(n_, s_, d_))

        s_tri = sb("s_tri", [128, 128])
        s_id = sb("s_id", [128, 128], BF16)
        s_onesb = sb("s_onesb", [2, 128], BF16)
        s_alibi = sb("s_alibi", [2, S], BF16)

        p_a = st.enter_context(nc.psum_tensor("p_a", [128, 2048], F32))
        p_b = st.enter_context(nc.psum_tensor("p_b", [128, 1536], F32))
        p_c = st.enter_context(nc.psum_tensor("p_c", [128, 8, 128], BF16))

        e_tri = P.dma("sy", lambda e: e.dma_start(out=s_tri[:, :], in_=tri_c))
        e_id = P.dma("sy", lambda e: e.dma_start(out=s_id[:, :], in_=ident_c))
        e_onb = P.dma("sy", lambda e: e.dma_start(out=s_onesb[:, :], in_=onesb_c))
        e_wqc = P.dma("gp", lambda e: e.dma_start(out=wqb_d, in_=wq_sl))
        e_c1c = P.dma("gp", lambda e: e.dma_start(out=c1tb_d, in_=c1t))

        def emit_ln(loader, storer, n, tag, lst):
            ln_x = lst.enter_context(nc.sbuf_tensor(f"lnx{tag}", [128, 2, D], F32))
            ln_sq = lst.enter_context(nc.sbuf_tensor(f"lnq{tag}", [128, D], BF16))
            ln_st = lst.enter_context(nc.sbuf_tensor(f"lns{tag}", [128, 2, 8], F32))
            ln_o = lst.enter_context(nc.sbuf_tensor(f"lno{tag}", [128, 2, D], BF16))
            out = []
            prev_done = [[], []]
            prev_sq = None
            for i in range(n):
                j = i % 2
                ld = loader(i, ln_x[:, j, :], prev_done[j])
                sqd = [ld] + ([prev_sq] if prev_sq else [])
                sq = P.ev("ac", (lambda j=j: lambda e: e.activation(
                    out=ln_sq[:, :], in_=ln_x[:, j, :], func=AF.Square,
                    accum_out=ln_st[:, j, 1:2]))(), deps=sqd)
                prev_sq = sq
                sm0 = P.ev("dv", (lambda j=j: lambda e: e.reduce_sum(
                    ln_st[:, j, 0:1], ln_x[:, j, :], axis=AX.X))(), deps=[ld])
                sm = P.ev("dv", (lambda j=j: lambda e: e.tensor_scalar(
                    out=ln_st[:, j, 2:3], in0=ln_st[:, j, 0:1],
                    scalar1=1.0 / D, scalar2=None, op0=ALU.mult))(), deps=[sm0])
                var0 = P.ev("dv", (lambda j=j: lambda e: (
                    e.tensor_tensor(out=ln_st[:, j, 6:7], in0=ln_st[:, j, 2:3],
                                    in1=ln_st[:, j, 2:3], op=ALU.mult),
                    e.tensor_scalar(out=ln_st[:, j, 7:8], in0=ln_st[:, j, 1:2],
                                    scalar1=1.0 / D, scalar2=EPS, op0=ALU.mult,
                                    op1=ALU.add),
                )[-1])(), deps=[sq, sm])
                var = P.ev("dv", (lambda j=j: lambda e: e.tensor_tensor(
                    out=ln_st[:, j, 3:4], in0=ln_st[:, j, 7:8],
                    in1=ln_st[:, j, 6:7], op=ALU.subtract))(), deps=[var0])
                rst0 = P.ev("ac", (lambda j=j: lambda e: e.activation(
                    out=ln_st[:, j, 4:5], in_=ln_st[:, j, 3:4],
                    func=AF.Ln, bias=0.0, scale=1.0))(), deps=[var])
                rstd = P.ev("ac", (lambda j=j: lambda e: e.activation(
                    out=ln_st[:, j, 5:6], in_=ln_st[:, j, 4:5],
                    func=AF.Exp, scale=-0.5))(), deps=[rst0])
                xo = P.ev("dv", (lambda j=j: lambda e: e.tensor_scalar(
                    out=ln_o[:, j, :], in0=ln_x[:, j, :],
                    scalar1=ln_st[:, j, 2:3], scalar2=ln_st[:, j, 5:6],
                    op0=ALU.subtract, op1=ALU.mult))(), deps=[rstd])
                toks = storer(i, ln_o[:, j, :], [xo])
                prev_done[j] = toks + [xo]
                out.append((toks, xo))
            return out

        def ld1(i, dst, war):
            return P.dma("sy", (lambda i=i, dst=dst: lambda e: e.dma_start(
                out=dst, in_=x_slab[i * 128:(i + 1) * 128, :]))(), deps=war)

        def st1(i, src, deps):
            return [P.dma("gp", (lambda i=i, src=src: lambda e: e.dma_start(
                out=xin_sl_d[i * 128:(i + 1) * 128, :], in_=src))(), deps=deps,
                sem="xns0" if i < 4 else "xns1")]

        with ExitStack() as lst1:
            ln1 = emit_ln(ld1, st1, NT_SL, "p1", lst1)

        cc_xin0 = P.cc(lambda g: g.collective_compute(
            "AllGather", ALU.bypass, replica_groups=[list(range(NC))],
            ins=[xin_sl_d[0:TS // 2, :]], outs=[xin_fu0_d]),
            deps=[t for tt, _ in ln1[:4] for t in tt], tag="ag_xin0")
        cc_xin1 = P.cc(lambda g: g.collective_compute(
            "AllGather", ALU.bypass, replica_groups=[list(range(NC))],
            ins=[xin_sl_d[TS // 2:TS, :]], outs=[xin_fu1_d]),
            deps=[t for tt, _ in ln1[4:] for t in tt], tag="ag_xin1")

        def finish(last_deps, dbg_src=None, dbg_bf16=True):
            toks = list(last_deps)
            if dbg_src is not None:
                toks = [P.dma("gp", lambda e: e.dma_start(out=dbg_d, in_=dbg_src),
                              deps=toks)]
                toks = [P.dma("sy", lambda e: e.dma_start(out=out_slab, in_=dbg_d),
                              deps=toks)]
            self_tok = P.ev("sy", lambda e: None, deps=toks, tag="final")
            with nc.Block() as blk:
                P.run(blk)

        if STAGE == "xin":
            finish([cc_xin0, cc_xin1], dbg_src=xin_sl_d)
            return nc

        last_pv_of_batch = None
        ctx_store_toks = []

        with (
            nc.sbuf_tensor("wq_sb", [128, 32 * W3], BF16) as wq_sb,
            nc.sbuf_tensor("s_bqb", [1, W3], BF16) as s_bqb,
            nc.sbuf_tensor("xin_s", [128, 32, 512], BF16) as xin_s,
            nc.sbuf_tensor("qkv_sb", [128, 16, W3], BF16) as qkv_sb,
            nc.sbuf_tensor("sdg", [128, 128], F32) as s_diag,
            nc.sbuf_tensor("ssc", [128, S], F32) as s_sc,
            nc.sbuf_tensor("prb", [128, S], BF16) as probs,
            nc.sbuf_tensor("qk_dm", [128, 2, S], BF16) as qk_dm,
            nc.sbuf_tensor("pT", [128, 8, 128], BF16) as pT_sb,
            nc.sbuf_tensor("rdg", [128, 2, 128], BF16) as rd_sb,
            nc.sbuf_tensor("cxe", [128, 2, 128], BF16) as ctx_sb,
            nc.sbuf_tensor("astt", [128, 2, 8], F32) as stat,
        ):
            e_bq = P.dma("gp", lambda e: e.dma_start(out=s_bqb[:, :], in_=bq_sl),
                         deps=[cc_xin0])
            e_wql = P.dma("sy", lambda e: e.dma_start(
                out=wq_sb[:, :].rearrange("p (c n) -> p c n", n=W3),
                in_=wqb_d.rearrange("(c p) n -> p c n", p=128)),
                deps=[e_wqc, cc_xin0])
            prev_scores = None
            prev_exps = None
            prev_evac = None
            prev_sc_user = None
            prev_ctx_evac = None
            prev_mx0 = None
            prev_rdg_user = None
            st_pv = {"cx": None, "last_ptr": None, "last_pv": None,
                     "rdg_user": None}
            prev_subs = [None] * 8
            e_alh = None
            for b in range(B):
                qkv_evac = []
                prev_half_lastmm = None
                for half in range(4):
                    gt = b * S + half * 512
                    loads = []
                    if half % 2 == 0:
                        xsrc0, ccx = xin_fu0_d, cc_xin0
                        ro = gt // 2
                    else:
                        xsrc0, ccx = xin_fu1_d, cc_xin1
                        ro = (gt - 512) // 2
                    for f in range(32):
                        dep = [ccx, e_wql]
                        if prev_subs[f // 4] is not None:
                            dep.append(prev_subs[f // 4])
                        loads.append(P.dma("sy", (lambda f=f, ro=ro,
                                     xsrc0=xsrc0: lambda e:
                                     e.dma_start(out=xin_s[:, f, :],
                                                 in_=xsrc0[ro:ro + 512,
                                                           f * 128:(f + 1) * 128],
                                                 transpose=True))(), deps=dep))
                    sub_last = [None] * 8
                    for tc4 in range(4):
                        gtc = half * 4 + tc4
                        subs = []
                        for k in range(8):
                            deps = loads[k * 4:(k + 1) * 4]
                            if k == 0:
                                deps = deps + [e_onb]
                                if len(qkv_evac) >= 2:
                                    deps.append(qkv_evac[-2])
                                if last_pv_of_batch is not None:
                                    deps.append(last_pv_of_batch)
                                if prev_ctx_evac is not None:
                                    deps.append(prev_ctx_evac)
                                if prev_evac is not None:
                                    deps.append(prev_evac)
                                if prev_mx0 is not None:
                                    deps.append(prev_mx0)
                            else:
                                deps = deps + [subs[-1]]
                            if k == 7:
                                deps = deps + [e_bq]

                            def emit_qs(e, tc4=tc4, gtc=gtc, k=k):
                                pq = p_a if gtc % 2 == 0 else p_b
                                mm = None
                                for f in range(k * 4, (k + 1) * 4):
                                    for o in range(0, W3, 512):
                                        mm = e.matmul(
                                            pq[:, o:o + 512],
                                            lhsT=xin_s[:, f,
                                                       tc4 * 128:
                                                       (tc4 + 1) * 128],
                                            rhs=wq_sb[:, f * W3 + o:
                                                      f * W3 + o + 512],
                                            start=(f == 0), stop=False)
                                if k == 7:
                                    for o in range(0, W3, 512):
                                        mm = e.matmul(
                                            pq[:, o:o + 512],
                                            lhsT=s_onesb[0:1, 0:128],
                                            rhs=s_bqb[:, o:o + 512],
                                            start=False, stop=True)
                                return mm
                            sub = P.ev("pe", emit_qs, deps=deps)
                            subs.append(sub)
                            sub_last[k] = sub
                        mm = subs[-1]
                        prev_half_lastmm = mm
                        ev = P.ev("dv", (lambda gtc=gtc: lambda e: (
                            e.tensor_scalar(out=qkv_sb[:, gtc, 0:HPC * HD],
                                            in0=(p_a if gtc % 2 == 0 else p_b)[:, 0:HPC * HD],
                                            scalar1=SCALE,
                                            scalar2=None, op0=ALU.mult),
                            e.tensor_copy(out=qkv_sb[:, gtc, HPC * HD:W3],
                                          in_=(p_a if gtc % 2 == 0 else p_b)[:, HPC * HD:W3]),
                        )[-1])(), deps=[mm])
                        qkv_evac.append(ev)
                    prev_subs = list(sub_last)

                for h in range(HPC):
                    aldeps = [prev_scores] if prev_scores is not None else []
                    e_alh = P.dma("sy", (lambda h=h: lambda e: e.dma_start(
                        out=s_alibi[:, :], in_=alibi_r[h, :, :]))(), deps=aldeps)
                    tr_copies = []
                    prev_cp = None
                    for tc in range(16):
                        for qk in range(2):
                            col = qk * HPC * HD + h * HD
                            trdeps = [qkv_evac[tc], e_id]
                            if prev_cp is not None:
                                trdeps.append(prev_cp)
                            tr = P.ev("pe", (lambda tc=tc, col=col: lambda e:
                                      e.transpose(p_c[:, 0, :],
                                                  in_=qkv_sb[:, tc, col:col + HD],
                                                  identity=s_id[:, :]))(),
                                      deps=trdeps)
                            cpdeps = [tr]
                            if tc == 0 and qk == 0 and prev_scores is not None:
                                cpdeps.append(prev_scores)
                            cp = P.ev("dv", (lambda tc=tc, qk=qk: lambda e:
                                      e.tensor_copy(
                                          out=qk_dm[:, qk, tc * 128:(tc + 1) * 128],
                                          in_=p_c[:, 0, :]))(), deps=cpdeps)
                            prev_cp = cp
                            tr_copies.append(cp)
                    def emit_pv(pend):
                        qcp, rdiag_tok = pend
                        vcol = 2 * HPC * HD + h * HD
                        nkc = qcp + 1
                        last = {"tr": None, "cp": None, "pv": None}
                        trs = []
                        cps = []
                        pvs = []
                        PVG = int(os.environ.get("MK_PVG", "2"))
                        for g0 in range(0, nkc, PVG):
                            gn = min(PVG, nkc - g0)
                            for i in range(gn):
                                kc = g0 + i
                                trdeps = [rdiag_tok]
                                if kc >= 8:
                                    trdeps.append(cps[kc - 8])
                                tr = P.ev("pe", (lambda kc=kc: lambda e:
                                          e.transpose(
                                              p_c[:, kc % 8, :],
                                              in_=probs[:, kc * 128:(kc + 1) * 128],
                                              identity=s_id[:, :]))(),
                                          deps=trdeps)
                                trs.append(tr)
                            for i in range(gn):
                                kc = g0 + i
                                cpdeps = [trs[kc]]
                                if kc >= 8:
                                    cpdeps.append(pvs[kc - 8])
                                cp = P.ev("dv", (lambda kc=kc: lambda e:
                                            e.tensor_copy(
                                                out=pT_sb[:, kc % 8, :],
                                                in_=p_c[:, kc % 8, :]))(),
                                            deps=cpdeps)
                                cps.append(cp)
                            for i in range(gn):
                                kc = g0 + i
                                pvdeps = [cps[kc]]
                                if kc == 0 and st_pv["cx"] is not None:
                                    pvdeps.append(st_pv["cx"])
                                pv = P.ev("pe", (lambda kc=kc, qcp=qcp, vcol=vcol:
                                          lambda e: e.matmul(
                                              p_b[:, 0:128],
                                              lhsT=qkv_sb[:, kc, vcol:vcol + HD],
                                              rhs=pT_sb[:, kc % 8, :],
                                              start=(kc == 0),
                                              stop=(kc == qcp)))(),
                                          deps=pvdeps)
                                pvs.append(pv)
                        cxdeps = [pvs[-1]]
                        if len(ctx_store_toks) >= 2:
                            cxdeps.append(ctx_store_toks[-2])
                        cx = P.ev("dv", (lambda qcp=qcp: lambda e: e.tensor_copy(
                            out=ctx_sb[:, qcp % 2, :], in_=p_b[:, 0:128]))(),
                            deps=cxdeps)
                        st_pv["cx"] = cx
                        st_pv["last_ptr"] = trs[-1]
                        st_pv["last_pv"] = pvs[-1]
                        st_pv["rdg_user"] = trs[-1]
                        ctx_store_toks.append(P.dma("gp", (
                            lambda h=h, qcp=qcp, b=b: lambda e: e.dma_start(
                                out=ctxT_d[h * HD:(h + 1) * HD,
                                           b * S + qcp * 128:
                                           b * S + (qcp + 1) * 128],
                                in_=ctx_sb[:, qcp % 2, :]))(), deps=[cx]))

                    pending = None
                    for qc in range(NQC):
                        pfx = (qc + 1) * 128
                        so = qc % 2
                        deps = [tr_copies[-1], e_onb, e_alh]
                        if prev_evac is not None:
                            deps.append(prev_evac)
                        if prev_mx0 is not None:
                            deps.append(prev_mx0)

                        def emit_scores(e, qc=qc, pfx=pfx, h=h):
                            o = 0
                            while o < pfx:
                                nn = min(512, pfx - o)
                                e.matmul(p_a[:, o:o + nn],
                                         lhsT=qk_dm[:, 0, qc * 128:(qc + 1) * 128],
                                         rhs=qk_dm[:, 1, o:o + nn],
                                         start=True, stop=False)
                                o += nn
                            o = 0
                            mm = None
                            while o < pfx:
                                nn = min(512, pfx - o)
                                mm = e.matmul(p_a[:, o:o + nn],
                                              lhsT=s_onesb[0:2, 0:128],
                                              rhs=s_alibi[0:2, o:o + nn],
                                              start=False, stop=True)
                                o += nn
                            return mm
                        sc = P.ev("pe", emit_scores, deps=deps)
                        prev_scores = sc
                        dgdeps = [sc, e_tri]
                        if prev_exps is not None:
                            dgdeps.append(prev_exps)
                        dg = P.ev("dv", (lambda qc=qc: lambda e: e.tensor_tensor(
                            out=s_diag[:, :], in0=p_a[:, qc * 128:(qc + 1) * 128],
                            in1=s_tri[:, :], op=ALU.add))(), deps=dgdeps)
                        if qc > 0:
                            sevdeps = [sc]
                            if prev_sc_user is not None:
                                sevdeps.append(prev_sc_user)
                            sev = P.ev("dv", (lambda qc=qc: lambda e:
                                       e.tensor_copy(out=s_sc[:, 0:qc * 128],
                                                     in_=p_a[:, 0:qc * 128]))(),
                                       deps=sevdeps)
                        else:
                            sev = dg
                        prev_evac = sev
                        if pending is not None:
                            emit_pv(pending)
                            pending = None
                        if qc > 0:
                            mx0 = P.ev("dv", (lambda qc=qc, so=so: lambda e: (
                                e.reduce_max(stat[:, so, 0:1], p_a[:, 0:qc * 128],
                                             axis=AX.X, negate=True),
                                e.reduce_max(stat[:, so, 1:2], s_diag[:, :],
                                             axis=AX.X, negate=True),
                            )[-1])(), deps=[dg, sc])
                            mx = P.ev("dv", (lambda so=so: lambda e:
                                      e.tensor_tensor(out=stat[:, so, 2:3],
                                                      in0=stat[:, so, 0:1],
                                                      in1=stat[:, so, 1:2],
                                                      op=ALU.min))(), deps=[mx0])
                        else:
                            mx0 = P.ev("dv", (lambda so=so: lambda e: e.reduce_max(
                                stat[:, so, 2:3], s_diag[:, :], axis=AX.X,
                                negate=True))(), deps=[dg])
                            mx = mx0
                        prev_mx0 = mx0
                        exdeps = [mx]
                        if st_pv["last_ptr"] is not None:
                            exdeps.append(st_pv["last_ptr"])
                        prev_rdg_user = st_pv["rdg_user"]

                        def emit_exps(e, qc=qc, pfx=pfx, so=so):
                            if qc > 0:
                                e.activation(out=probs[:, 0:qc * 128],
                                             in_=s_sc[:, 0:qc * 128], func=AF.Exp,
                                             bias=stat[:, so, 2:3], scale=1.0,
                                             accum_out=stat[:, so, 3:4])
                            return e.activation(out=probs[:, qc * 128:pfx],
                                                in_=s_diag[:, :], func=AF.Exp,
                                                bias=stat[:, so, 2:3], scale=1.0,
                                                accum_out=stat[:, so, 4:5])
                        ex = P.ev("ac", emit_exps, deps=exdeps)
                        prev_exps = ex
                        prev_sc_user = ex

                        def emit_rsum(e, qc=qc, so=so):
                            if qc > 0:
                                return e.tensor_tensor(out=stat[:, so, 5:6],
                                                       in0=stat[:, so, 3:4],
                                                       in1=stat[:, so, 4:5],
                                                       op=ALU.add)
                            return e.tensor_copy(out=stat[:, so, 5:6],
                                                 in_=stat[:, so, 4:5])
                        rsum = P.ev("dv", emit_rsum, deps=[ex])
                        rcp = P.ev("dv", (lambda so=so: lambda e: e.reciprocal(
                            stat[:, so, 6:7], stat[:, so, 5:6]))(), deps=[rsum])
                        psdeps = [rcp]
                        if prev_rdg_user is not None:
                            psdeps.append(prev_rdg_user)
                        pscale = P.ev("dv", (lambda pfx=pfx, so=so: lambda e:
                                     e.tensor_scalar(out=probs[:, 0:pfx],
                                                     in0=probs[:, 0:pfx],
                                                     scalar1=stat[:, so, 6:7],
                                                     scalar2=None,
                                                     op0=ALU.mult))(),
                                     deps=psdeps)
                        pending = (qc, pscale)
                    emit_pv(pending)
                    pending = None
                    prev_ctx_evac = st_pv["cx"]
                    last_pv_of_batch = st_pv["last_pv"]

        e_w1c = P.dma("gp", lambda e: e.dma_start(out=w1b_d, in_=w1_sl),
                      sem="w1cs")
        e_w2c = P.dma("gp", lambda e: e.dma_start(out=w2b_d, in_=w2_sl),
                      sem="w2cs")

        if STAGE == "attn":
            finish([ctx_store_toks[-1]],
                   dbg_src=ctxT_d.rearrange("a (b c) -> (a b) c", c=D))
            return nc

        att_stores = []
        with (
            nc.sbuf_tensor("wo_sb", [128, HPC, D], BF16) as wo_sb,
            nc.sbuf_tensor("cxt", [128, 2, HPC, 128], BF16) as cxt,
            nc.sbuf_tensor("aop", [128, 2, D], BF16) as aop,
        ):
            wo_lds = [P.dma("gp", (lambda hh=hh: lambda e: e.dma_start(
                out=wo_sb[:, hh, :], in_=wo_sl[hh * 128:(hh + 1) * 128, :]))(),
                deps=[ctx_store_toks[-1]])
                for hh in range(HPC)]
            prev_mm = [None, None]
            prev_ev1 = [None, None]
            P4ORD = [tc for ph in (0, 1) for tc in range(NTC)
                     if (tc % 8) // 4 == ph]
            cc_att0 = None
            for kk, tc in enumerate(P4ORD):
                j = kk % 2
                cld = []
                for hh in range(HPC):
                    dep = [ctx_store_toks[-1]]
                    if prev_mm[j] is not None:
                        dep.append(prev_mm[j])
                    cld.append(P.dma("sy", (lambda tc=tc, hh=hh, j=j: lambda e:
                               e.dma_start(out=cxt[:, j, hh, :],
                                           in_=ctxT_d[hh * 128:(hh + 1) * 128,
                                                      tc * 128:(tc + 1) * 128]))(),
                               deps=dep))
                gevs = []
                mm1 = None
                for g in range(8):
                    po = (g % 4) * 512
                    deps = cld + wo_lds
                    if len(gevs) >= 4:
                        deps.append(gevs[g - 4])
                    elif prev_ev1[0] is not None:
                        deps.append(prev_ev1[0][g % 4])

                    def emit_op(e, g=g, j=j, po=po):
                        mm = None
                        for hh in range(HPC):
                            mm = e.matmul(p_a[:, po:po + 512],
                                          lhsT=cxt[:, j, hh, :],
                                          rhs=wo_sb[:, hh,
                                                    g * 512:(g + 1) * 512],
                                          start=(hh == 0),
                                          stop=(hh == HPC - 1))
                        return mm
                    mm1 = P.ev("pe", emit_op, deps=deps)
                    evdeps = [mm1]
                    if len(att_stores) >= 2:
                        evdeps.append(att_stores[-2])
                    gevs.append(P.ev("dv", (lambda g=g, j=j, po=po: lambda e:
                              e.tensor_copy(
                                  out=aop[:, j, g * 512:(g + 1) * 512],
                                  in_=p_a[:, po:po + 512]))(), deps=evdeps))
                ev0, ev1 = gevs[0], gevs[-1]
                prev_mm[j] = mm1
                prev_ev1[0] = gevs[4:8]
                dst = (att_pt0_d[kk * 128:(kk + 1) * 128, :] if kk < 32 else
                       att_pt1_d[(kk - 32) * 128:(kk - 31) * 128, :])
                att_stores.append(P.dma("sy", (lambda dst=dst, j=j: lambda e:
                                  e.dma_start(out=dst,
                                              in_=aop[:, j, :]))(),
                                  deps=[ev0, ev1], sem="ats0" if kk < 32 else "ats1"))
                if kk == 31:
                    cc_att0 = P.cc(lambda g: g.collective_compute(
                        "ReduceScatter", ALU.add,
                        replica_groups=[list(range(NC))],
                        ins=[att_pt0_d], outs=[att_rs0_d]),
                        deps=list(att_stores), tag="rs_att0")

        if STAGE == "outproj":
            finish([att_stores[-1]], dbg_src=att_pt0_d[0:TS, :])
            return nc

        cc_att1 = P.cc(lambda g: g.collective_compute(
            "ReduceScatter", ALU.add, replica_groups=[list(range(NC))],
            ins=[att_pt1_d], outs=[att_rs1_d]), deps=att_stores[32:],
            tag="rs_att1")

        s_c2t = sb("s_c2t", [RANK, SLICE], BF16)
        s_c1tb = sb("s_c1tb", [128, 32 * RANK], BF16)
        s_iotan = sb("s_iotan", [128, SPB])
        s_blkoff = sb("s_blkoff", [128, BPC])
        h_sb = sb("h_sb", [128, BPC * NTC])
        idx1_sb = sb("idx1_sb", [128, BPC * NTC], I32)
        idx1f_sb = sb("idx1f_sb", [128, BPC * NTC])
        idx2_sb = sb("idx2_sb", [128, BPC * NTC], I32)
        e_c2 = P.dma("gp", lambda e: e.dma_start(out=s_c2t[:, :], in_=c2t_sl),
                     deps=[cc_att0, att_stores[-1]])
        e_c1l = P.dma("gp", lambda e: e.dma_start(
            out=s_c1tb[:, :].rearrange("p (c r) -> p c r", r=RANK),
            in_=c1tb_d.rearrange("(c p) r -> p c r", p=128)),
            deps=[e_c1c, cc_att0, att_stores[-1]])
        e_io1 = P.ev("gp", lambda e: e.iota(
            s_iotan[:, :], pattern=[[-1, SPB]], base=SPB - 1, channel_multiplier=0,
            allow_small_or_imprecise_dtypes=True),
            deps=[cc_att0, att_stores[-1]])
        e_io2a = P.ev("gp", lambda e: e.iota(
            s_blkoff[:, :], pattern=[[SPB, BPC]], base=0, channel_multiplier=0,
            allow_small_or_imprecise_dtypes=True),
            deps=[cc_att0, att_stores[-1]])
        e_io2 = P.ev("dv", lambda e: e.tensor_scalar(
            out=s_blkoff[:, :], in0=s_blkoff[:, :], scalar1=float(D),
            scalar2=float((SPB - 1) * D), op0=ALU.mult, op1=ALU.add),
            deps=[e_io2a])

        if STAGE == "attsum":
            d1 = P.dma("gp", lambda e: e.dma_start(
                out=dbg_d[0:TS // 2, :], in_=att_rs0_d), deps=[cc_att0])
            d2 = P.dma("gp", lambda e: e.dma_start(
                out=dbg_d[TS // 2:TS, :], in_=att_rs1_d), deps=[cc_att1])
            finish([d1, d2], dbg_src=dbg_d)
            return nc

        with (
            nc.sbuf_tensor("res_a", [128, 2, D], BF16) as res_a,
            nc.sbuf_tensor("s_bo", [128, D], F32) as s_bo,
            nc.sbuf_tensor("xf_fm", [128, 32, TS], BF16) as xf_fm,
            nc.sbuf_tensor("r_sb", [RANK, TS], BF16) as r_sb,
            ExitStack() as lst2,
        ):
            p4_guard = att_stores[-1]
            e_bo = P.dma("sy", lambda e: e.dma_start(out=s_bo[:, :], in_=bout_bc),
                         deps=[cc_att0, p4_guard])
            tr_last = {}
            all_tr = []
            xf_stores = []

            def ld2(i, dst, war):
                j = i % 2
                lx = P.dma("sy", (lambda i=i, dst=dst: lambda e: e.dma_start(
                    out=dst, in_=x_slab[i * 128:(i + 1) * 128, :]))(),
                    deps=war + [p4_guard])
                rs_src = (att_rs0_d[i * 128:(i + 1) * 128, :] if i < 4 else
                          att_rs1_d[(i - 4) * 128:(i - 3) * 128, :])
                la = P.dma("sy", (lambda src=rs_src, j=j: lambda e:
                    e.dma_start(out=res_a[:, j, :], in_=src))(),
                    deps=[cc_att0 if i < 4 else cc_att1, p4_guard] + war)
                sm0 = P.ev("dv", (lambda j=j, dst=dst: lambda e:
                           e.tensor_tensor(out=dst, in0=dst,
                                           in1=res_a[:, j, :], op=ALU.add))(),
                           deps=[lx, la])
                sm = P.ev("dv", (lambda j=j, dst=dst: lambda e:
                          e.tensor_tensor(out=dst, in0=dst,
                                          in1=s_bo[:, :], op=ALU.add))(),
                          deps=[sm0, e_bo])
                P.dma("sy", (lambda i=i, dst=dst: lambda e: e.dma_start(
                    out=xp_d[i * 128:(i + 1) * 128, :], in_=dst))(), deps=[sm])
                return sm

            def st2(i, src, deps):
                toks = [P.dma("gp", (lambda i=i, src=src: lambda e: e.dma_start(
                    out=xf_sl_d[i * 128:(i + 1) * 128, :], in_=src))(),
                    deps=deps, sem="xfs0" if i < 4 else "xfs1")]
                xf_stores.append(toks[0])
                for ch in range(32):
                    trdeps = list(deps) + [e_id]
                    if "t" in tr_last:
                        trdeps.append(tr_last["t"])
                    tr = P.ev("pe", (lambda ch=ch, src=src: lambda e: e.transpose(
                        p_c[:, 0, :], in_=src[:, ch * 128:(ch + 1) * 128],
                        identity=s_id[:, :]))(), deps=trdeps)
                    cp = P.ev("dv", (lambda ch=ch, i=i: lambda e: e.tensor_copy(
                        out=xf_fm[:, ch, i * 128:(i + 1) * 128],
                        in_=p_c[:, 0, :]))(), deps=[tr])
                    tr_last["t"] = cp
                    all_tr.append(cp)
                toks.append(tr_last["t"])
                return toks

            ln2 = emit_ln(ld2, st2, NT_SL, "p6", lst2)

            def emit_rproj(ph):
                def f(e, ph=ph):
                    mm = None
                    for ch in range(32):
                        mm = e.matmul(p_b[0:RANK, ph * 512:ph * 512 + 512],
                                      lhsT=s_c1tb[:, ch * RANK:(ch + 1) * RANK],
                                      rhs=xf_fm[:, ch,
                                                ph * 512:ph * 512 + 512],
                                      start=(ch == 0), stop=(ch == 31))
                    return mm
                return f
            rmm0 = P.ev("pe", emit_rproj(0),
                        deps=[all_tr[32 * 4 - 1], e_c1l, prev_ctx_evac])
            rev0 = P.ev("dv", lambda e: e.tensor_copy(
                out=r_sb[:, 0:512], in_=p_b[0:RANK, 0:512]), deps=[rmm0])
            rstr0 = P.dma("gp", lambda e: e.dma_start(
                out=r_sl0_d, in_=r_sb[:, 0:512]), deps=[rev0], sem="rss")
            rmm1 = P.ev("pe", emit_rproj(1), deps=[all_tr[-1], e_c1l])
            rev1 = P.ev("dv", lambda e: e.tensor_copy(
                out=r_sb[:, 512:1024], in_=p_b[0:RANK, 512:1024]),
                deps=[rmm1])
            rstr1 = P.dma("gp", lambda e: e.dma_start(
                out=r_sl1_d, in_=r_sb[:, 512:1024]), deps=[rev1], sem="rss")

        cc_xf0 = P.cc(lambda g: g.collective_compute(
            "AllGather", ALU.bypass, replica_groups=[list(range(NC))],
            ins=[xf_sl_d[0:TS // 2, :]], outs=[xf_fu0_d]),
            deps=xf_stores[:4], tag="ag_xf0")
        cc_r0 = P.cc(lambda g: g.collective_compute(
            "AllGather", ALU.bypass, replica_groups=[list(range(NC))],
            ins=[r_sl0_d], outs=[r_st0_d]), deps=[rstr0], tag="ag_r0")
        cc_xf1 = P.cc(lambda g: g.collective_compute(
            "AllGather", ALU.bypass, replica_groups=[list(range(NC))],
            ins=[xf_sl_d[TS // 2:TS, :]], outs=[xf_fu1_d]),
            deps=xf_stores[4:], tag="ag_xf1")
        cc_r1 = P.cc(lambda g: g.collective_compute(
            "AllGather", ALU.bypass, replica_groups=[list(range(NC))],
            ins=[r_sl1_d], outs=[r_st1_d]), deps=[rstr1], tag="ag_r1")

        if STAGE == "xf":
            finish([cc_xf0, cc_xf1, cc_r0, cc_r1], dbg_src=xf_sl_d)
            return nc

        with (
            nc.sbuf_tensor("r_t", [RANK, 4, 128], BF16) as r_t,
            nc.sbuf_tensor("amx", [128, BPC], F32) as amx,
            nc.sbuf_tensor("amall", [128, SLICE], BF16) as amall,
            nc.sbuf_tensor("amt", [128, BPC * 4], F32) as amt,
            nc.sbuf_tensor("xf_t", [128, 2, D], BF16) as xf_t,
            nc.sbuf_tensor("g1", [128, 7, D], BF16) as g1,
            nc.sbuf_tensor("pmul", [128, 2, D], BF16) as pmul,
            nc.sbuf_tensor("hacc", [128, 2, BPC], F32) as hacc,
            nc.sbuf_tensor("g2", [128, BPC, D], BF16) as g2,
            nc.sbuf_tensor("hd", [128, 2 * BPC, 128], BF16) as hd,
            nc.sbuf_tensor("fo", [128, D], BF16) as fo,
        ):
            W2G = [(o, 512) for o in range(0, D, 512)]
            TORD = [tt for ph in (0, 1) for tt in range(NTC)
                    if (tt % 8) // 4 == ph]
            am_toks = [None] * (NTC + 1)
            a2_list = [None] * (NTC + 1)
            imm_list = [None] * (NTC + 2)
            rl_list = [None] * (NTC + 4)
            xl_list = [None] * (NTC + 2)
            h_toks = [None] * NTC
            mul_list = []
            red_list = []
            gt1s = {}
            g2_list = [None] * NTC
            hd_list = [None] * NTC
            wmm_list = [None] * NTC
            ev_list = [None] * NTC
            st_list = [None] * NTC
            ffn_stores = []

            def emit_rl(t):
                if t >= NTC:
                    return
                tl = TORD[t]
                dep = [cc_r1 if t < 6 else (cc_r0 if t < 32 else cc_r1)]
                if t >= 4 and imm_list[t - 4] is not None:
                    dep.append(imm_list[t - 4])
                rsrc = (r_st0_d[RANK * (tl // 8):RANK * (tl // 8) + RANK,
                                128 * (tl % 8):128 * (tl % 8) + 128]
                        if t < 32 else
                        r_st1_d[RANK * (tl // 8):RANK * (tl // 8) + RANK,
                                128 * (tl % 8 - 4):128 * (tl % 8 - 4) + 128])
                rl_list[t] = P.dma("sy", (lambda t=t, rsrc=rsrc: lambda e:
                    e.dma_start(out=r_t[:, t % 4, :], in_=rsrc))(),
                    deps=dep, sem="rls")

            def emit_imm(t):
                if t >= NTC:
                    return
                deps = [rl_list[t], e_c2]
                if t >= 1:
                    deps.append(a2_list[t - 1])

                def f(e, j=t % 4):
                    mm = None
                    for o in range(0, SLICE, 512):
                        nn = min(512, SLICE - o)
                        mm = e.matmul(p_a[:, o:o + nn], lhsT=r_t[:, j, :],
                                      rhs=s_c2t[:, o:o + nn], start=True,
                                      stop=True)
                    return mm
                imm_list[t] = P.ev("pe", f, deps=deps)

            def emit_xl(t):
                if t >= NTC:
                    return
                tl = TORD[t]
                dep = [cc_r1 if t < 4 else (cc_xf0 if t < 32 else cc_xf1)]
                if t >= 2:
                    dep.append(mul_list[(t - 1) * BPC - 1])
                ro = ((tl // 8) * 512 + (tl % 8) * 128 if t < 32 else
                      (tl // 8) * 512 + (tl % 8 - 4) * 128)
                xsrc = (xf_fu0_d[ro:ro + 128, :] if t < 32 else
                        xf_fu1_d[ro:ro + 128, :])
                xl_list[t] = P.dma("sy", (lambda t=t, xsrc=xsrc: lambda e:
                    e.dma_start(out=xf_t[:, t % 2, :], in_=xsrc))(), deps=dep,
                    sem="xls")

            def emit_argmax(t):
                if t >= NTC:
                    return
                a1 = P.ev("dv", lambda e: e.reduce_max(
                    amx[:, :], p_a[:, 0:SLICE].rearrange("p (b s) -> p b s",
                                                         s=SPB), axis=AX.X),
                    deps=[imm_list[t]])

                def emit_eq(e):
                    mm = None
                    for bb in range(BPC):
                        mm = e.scalar_tensor_tensor(
                            out=amall[:, bb * SPB:(bb + 1) * SPB],
                            in0=p_a[:, bb * SPB:(bb + 1) * SPB],
                            scalar=amx[:, bb:bb + 1], in1=s_iotan[:, :],
                            op0=ALU.is_equal, op1=ALU.mult)
                    return mm
                a2 = P.ev("dv", emit_eq, deps=[a1, e_io1])
                a2_list[t] = a2
                a3 = P.ev("dv", lambda e: e.reduce_max(
                    amt[:, 0:BPC],
                    amall[:, :].rearrange("p (b s) -> p b s", s=SPB),
                    axis=AX.X), deps=[a2])
                a6 = P.ev("dv", (lambda t=t: lambda e:
                    e.scalar_tensor_tensor(
                        out=idx1f_sb[:, t * BPC:(t + 1) * BPC],
                        in0=amt[:, 0:BPC], scalar=-float(D),
                        in1=s_blkoff[:, :],
                        op0=ALU.mult, op1=ALU.add))(), deps=[a3, e_io2])
                am_toks[t] = P.ev("dv", (lambda t=t: lambda e: e.tensor_copy(
                    out=idx1_sb[:, t * BPC:(t + 1) * BPC],
                    in_=idx1f_sb[:, t * BPC:(t + 1) * BPC]))(), deps=[a6])

            def emit_g1(t, bb, war):
                if t >= NTC:
                    return
                pi = t * BPC + bb
                agd = [cc_r1] if t < 4 else []
                gt1s[(t, bb)] = P.dma("gp", (lambda t=t, bb=bb, j2=pi % 7:
                           lambda e: e.indirect_dma_start(
                               out=g1[:, j2, :], out_offset=None,
                               in_=w1b_d.rearrange("(x p) d -> x (p d)",
                                                   x=128),
                               in_offset=IndirectOffsetOnAxis(
                                   idx1_sb[:, t * BPC + bb:t * BPC + bb + 1],
                                   axis=1),
                               bounds_check=None))(),
                           deps=[am_toks[t], e_w1c] + war + agd, sem="g1s")

            def emit_g2bb(t, bb):
                if t < 0:
                    return
                gdeps2 = [am_toks[t], e_w2c]
                if t < 4:
                    gdeps2.append(cc_r1)
                if t >= 1 and wmm_list[t - 1] is not None:
                    gdeps2.append(wmm_list[t - 1])
                if g2_list[t] is None:
                    g2_list[t] = []
                g2_list[t].append(P.dma("gp", (lambda t=t, bb=bb: lambda e:
                           e.indirect_dma_start(
                               out=g2[:, bb, :],
                               out_offset=None,
                               in_=w2b_d.rearrange("(x p) d -> x (p d)",
                                                   x=128),
                               in_offset=IndirectOffsetOnAxis(
                                   idx1_sb[:, t * BPC + bb:
                                           t * BPC + bb + 1],
                                   axis=1),
                               bounds_check=None))(), deps=gdeps2,
                           sem="g2s"))

            def emit_hd(t):
                hds = []
                for bb in range(BPC):
                    hdd = [h_toks[t], e_id]
                    if t >= 2 and wmm_list[t - 2] is not None:
                        hdd.append(wmm_list[t - 2])
                    hds.append(P.ev("dv", (lambda t1=t, bb=bb: lambda e:
                               e.tensor_scalar(
                                   out=hd[:, (t1 % 2) * BPC + bb, :],
                                   in0=s_id[:, :],
                                   scalar1=h_sb[:, t1 * BPC + bb:
                                                t1 * BPC + bb + 1],
                                   scalar2=None, op0=ALU.mult))(),
                               deps=hdd))
                hd_list[t] = hds

            def emit_w2grp(t, gi, evs):
                go, gw = W2G[gi]
                po = (gi % 2) * 512
                deps2 = g2_list[t] + hd_list[t]
                if gi >= 2:
                    deps2.append(evs[gi - 2])
                elif t >= 1 and ev_list[t - 1] is not None:
                    deps2.append(ev_list[t - 1][6 + gi])

                def f(e, go=go, gw=gw, t=t, po=po):
                    mm = None
                    for bb in range(BPC):
                        s2 = (t % 2) * BPC + bb
                        mm = e.matmul(p_b[:, po:po + gw],
                                      lhsT=hd[:, s2, :],
                                      rhs=g2[:, bb, go:go + gw],
                                      start=(bb == 0),
                                      stop=(bb == BPC - 1))
                    return mm
                wmm = P.ev("pe", f, deps=deps2)
                evdeps = [wmm]
                if t >= 1 and st_list[t - 1] is not None:
                    evdeps.append(st_list[t - 1])
                gev = P.ev("dv", (lambda go=go, gw=gw, po=po:
                           lambda e: e.tensor_copy(
                               out=fo[:, go:go + gw],
                               in_=p_b[:, po:po + gw]))(), deps=evdeps)
                evs.append(gev)
                return wmm

            for k in range(3):
                emit_rl(k)
            emit_imm(0)
            emit_xl(0)
            emit_xl(1)
            emit_argmax(0)
            emit_rl(3)
            emit_imm(1)
            for k in range(BPC):
                emit_g1(0, k, [])

            cc_ffn0_box = [None]

            def emit_w2tail(t1):
                evs = []
                wmm = None
                for gi in range(8):
                    wmm = emit_w2grp(t1, gi, evs)
                wmm_list[t1] = wmm
                ev_list[t1] = evs
                sdst = (ffn_pt0_d[t1 * 128:(t1 + 1) * 128, :] if t1 < 32
                        else ffn_pt1_d[(t1 - 32) * 128:(t1 - 31) * 128, :])
                st_list[t1] = P.dma("sy", (lambda sdst=sdst: lambda e:
                                 e.dma_start(out=sdst,
                                             in_=fo[:, :]))(), deps=evs,
                                 sem="sts0" if t1 < 32 else "sts1")
                ffn_stores.append(st_list[t1])

            for t in range(NTC):
                if t >= 1:
                    emit_hd(t - 1)
                emit_argmax(t + 1)
                emit_rl(t + 4)
                emit_imm(t + 2)
                reds = []
                for bb in range(BPC):
                    pi = t * BPC + bb
                    mldeps = [gt1s[(t, bb)], xl_list[t]]
                    if pi >= 2:
                        mldeps.append(red_list[pi - 2])
                    ml = P.ev("dv", (lambda j2=pi % 7, jp=pi % 2, j=t % 2:
                              lambda e:
                              e.tensor_mul(pmul[:, jp, :], g1[:, j2, :],
                                           xf_t[:, j, :]))(), deps=mldeps)
                    mul_list.append(ml)
                    rddeps = [ml]
                    if t >= 2:
                        rddeps.append(h_toks[t - 2])
                    rd = P.ev("ac", (lambda bb=bb, j2=pi % 2, hj=t % 2:
                              lambda e: e.activation(
                                  out=pmul[:, j2, :], in_=pmul[:, j2, :],
                                  func=AF.Copy,
                                  accum_out=hacc[:, hj, bb:bb + 1]))(),
                              deps=rddeps)
                    red_list.append(rd)
                    reds.append(rd)
                    if t >= 1:
                        emit_g2bb(t - 1, bb)
                hf = P.ev("dv", (lambda t=t, hj=t % 2: lambda e:
                    e.tensor_scalar(
                        out=h_sb[:, t * BPC:(t + 1) * BPC],
                        in0=hacc[:, hj, :], scalar1=0.0, scalar2=None,
                        op0=ALU.max))(), deps=reds)
                h_toks[t] = hf
                if t >= 1:
                    emit_w2tail(t - 1)
                emit_xl(t + 2)
                for k in range(BPC):
                    emit_g1(t + 1, k, [mul_list[t * BPC + k]])


            t = NTC
            emit_hd(NTC - 1)
            for bb in range(BPC):
                emit_g2bb(NTC - 1, bb)
            cc_ffn0_box[0] = P.cc(lambda g: g.collective_compute(
                "ReduceScatter", ALU.add,
                replica_groups=[list(range(NC))],
                ins=[ffn_pt0_d], outs=[ffn_rs0_d]),
                deps=st_list[0:32], tag="rs_ffn0")
            emit_w2tail(NTC - 1)
            am_toks = [x for x in am_toks if x is not None]
            h_toks = [x for x in h_toks if x is not None]

        if STAGE == "ffnp":
            finish([ffn_stores[-1]], dbg_src=ffn_pt0_d[0:TS, :])
            return nc
        if STAGE == "am":
            t1 = P.dma("gp", lambda e: e.dma_start(
                out=dbg_d[0:128, 0:BPC * NTC], in_=idx1f_sb[:, :]),
                deps=am_toks + h_toks)
            t2 = P.dma("gp", lambda e: e.dma_start(
                out=dbg_d[128:256, 0:BPC * NTC], in_=h_sb[:, :]),
                deps=am_toks + h_toks)
            t3 = P.dma("sy", lambda e: e.dma_start(out=out_slab, in_=dbg_d),
                       deps=[t1, t2])
            finish([t3])
            return nc

        cc_ffn0 = cc_ffn0_box[0]
        cc_ffn1 = P.cc(lambda g: g.collective_compute(
            "ReduceScatter", ALU.add, replica_groups=[list(range(NC))],
            ins=[ffn_pt1_d], outs=[ffn_rs1_d]), deps=ffn_stores[32:],
            tag="rs_ffn1")

        with (
            nc.sbuf_tensor("f_x", [128, 2, D], F32) as f_x,
            nc.sbuf_tensor("f_f", [128, 2, D], BF16) as f_f,
            nc.sbuf_tensor("f_o", [128, 2, D], F32) as f_o,
        ):
            fin_toks = []
            prev_add = [None, None]
            for i in range(NT_SL):
                j = i % 2
                war = [prev_add[j]] if prev_add[j] else []
                ccd = cc_ffn0 if i < 4 else cc_ffn1
                fsrc = (ffn_rs0_d[i * 128:(i + 1) * 128, :] if i < 4 else
                        ffn_rs1_d[(i - 4) * 128:(i - 3) * 128, :])
                lx = P.dma("sy", (lambda i=i, j=j: lambda e: e.dma_start(
                    out=f_x[:, j, :],
                    in_=xp_d[i * 128:(i + 1) * 128, :]))(), deps=war + [ccd])
                lf = P.dma("sy", (lambda fsrc=fsrc, j=j: lambda e:
                    e.dma_start(out=f_f[:, j, :], in_=fsrc))(),
                    deps=[ccd] + war)
                ad = P.ev("dv", (lambda j=j: lambda e: e.tensor_tensor(
                    out=f_o[:, j, :], in0=f_x[:, j, :], in1=f_f[:, j, :],
                    op=ALU.add))(), deps=[lx, lf])
                prev_add[j] = ad
                fin_toks.append(P.dma("sy", (lambda i=i, j=j: lambda e:
                                e.dma_start(
                                    out=out_slab[i * 128:(i + 1) * 128, :],
                                    in_=f_o[:, j, :]))(), deps=[ad]))

        finish(fin_toks)
        return nc

    return nc


_NC_CACHE = {}


def kernel(L0_attn_hidden_states, L0_attn_w_qkv, L0_attn_b_qkv, L0_attn_w_out,
           L0_attn_b_out, L0_ffn_c1_w, L0_ffn_c2_w, L0_ffn_w1, L0_ffn_w2t):
    x = np.ascontiguousarray(np.asarray(L0_attn_hidden_states,
                                        dtype=np.float32)).reshape(T, D)
    wq = np.asarray(L0_attn_w_qkv, dtype=np.float32)
    bq = np.asarray(L0_attn_b_qkv, dtype=np.float32)
    wo = np.asarray(L0_attn_w_out, dtype=np.float32)
    bo = np.asarray(L0_attn_b_out, dtype=np.float32)
    c1 = np.asarray(L0_ffn_c1_w, dtype=np.float32)
    c2 = np.asarray(L0_ffn_c2_w, dtype=np.float32)
    w1 = np.asarray(L0_ffn_w1, dtype=np.float32)
    w2 = np.asarray(L0_ffn_w2t, dtype=np.float32)

    slopes = get_alibi_slopes(H)
    pos = np.arange(S, dtype=np.float32)
    alibi = slopes[:, None] * pos[None, :]
    al_hi = alibi.astype(ml_dtypes.bfloat16)
    al_lo = (alibi - al_hi.astype(np.float32)).astype(ml_dtypes.bfloat16)
    al_hl = np.stack([al_hi, al_lo], axis=1)
    tri = np.where(np.arange(128)[None, :] > np.arange(128)[:, None],
                   np.float32(-1.0e9), np.float32(0.0)).astype(np.float32)
    ident = np.eye(128, dtype=ml_dtypes.bfloat16)
    onesb = np.ones((2, 128), dtype=ml_dtypes.bfloat16)
    bout = np.ascontiguousarray(np.broadcast_to(bo[None, :], (128, D)))
    c1t_host = np.ascontiguousarray(c1.T)

    in_maps = []
    for c in range(NC):
        hs = slice(c * HPC * HD, (c + 1) * HPC * HD)
        wq_c = np.ascontiguousarray(
            np.concatenate([wq[:, 0 * D:][:, hs], wq[:, 1 * D:][:, hs],
                            wq[:, 2 * D:][:, hs]], axis=1))
        bq_c = np.ascontiguousarray(
            np.concatenate([bq[0 * D:1 * D][hs], bq[1 * D:2 * D][hs],
                            bq[2 * D:3 * D][hs]])[None, :])
        rs = slice(c * SLICE, (c + 1) * SLICE)
        al_c = np.ascontiguousarray(al_hl[c * HPC:(c + 1) * HPC])
        in_maps.append({
            "x_slab": np.ascontiguousarray(x[c * TS:(c + 1) * TS]),
            "wq_sl": wq_c, "bq_sl": bq_c,
            "wo_sl": np.ascontiguousarray(wo[c * HPC * HD:(c + 1) * HPC * HD]),
            "bout_bc": bout,
            "c1t": c1t_host,
            "c2t_sl": np.ascontiguousarray(c2[rs].T),
            "w1_sl": np.ascontiguousarray(w1[rs]),
            "w2_sl": np.ascontiguousarray(w2[rs]),
            "alibi_r": al_c, "tri_c": tri, "ident_c": ident,
            "onesb_c": onesb,
        })

    key = STAGE
    if key not in _NC_CACHE:
        _NC_CACHE[key] = build()
    res = run_bass_kernel_spmd(_NC_CACHE[key], in_maps, core_ids=list(range(NC)))
    out = np.concatenate([np.asarray(r["out_slab"]) for r in res.results],
                         axis=0)
    return out.reshape(B, S, D).astype(np.float32)


if __name__ == "__main__":
    nc = build()
    print("build ok; instructions:",
          {k: len(v) for k, v in
           {e: [1] * 0 for e in ENGS}.items()} if False else "n/a")


# version 25
# speedup vs baseline: 1.0058x; 1.0058x over previous
import math
import os
import numpy as np
import ml_dtypes
import concourse.bass as bass
import concourse.mybir as mybir
from concourse.bass_utils import run_bass_kernel_spmd
from concourse.bass import IndirectOffsetOnAxis
from contextlib import ExitStack

F32 = mybir.dt.float32
BF16 = mybir.dt.bfloat16
I32 = mybir.dt.int32
AF = mybir.ActivationFunctionType
ALU = mybir.AluOpType
AX = mybir.AxisListType

NC = 8
B, S, D = 4, 2048, 4096
T = B * S
TS = T // NC
H, HD = 32, 128
HPC = H // NC
W3 = 3 * HPC * HD
INTER, RANK, SPB = 14336, 16, 256
NBLK = INTER // SPB
BPC = NBLK // NC
SLICE = BPC * SPB
SCALE = HD ** -0.5
EPS = 1e-5
BIGIDX = float(2 ** 23)
NQC = S // 128
NTC = T // 128
NT_SL = TS // 128
STAGE = os.environ.get("MK_STAGE", "full")

ENGS = ("sy", "pe", "dv", "ac", "gp")


class Prog:
    def __init__(self, nc, st):
        self.nc = nc
        self.events = {e: [] for e in ENGS}
        self.cnt = {e: 0 for e in ENGS}
        self.sems = {e: st.enter_context(nc.semaphore(f"sem_{e}")) for e in ENGS}
        self.ncc = 0
        self.st = st

    def xsem(self, name):
        if name not in self.sems:
            self.sems[name] = self.st.enter_context(
                self.nc.semaphore(f"sem_{name}"))
            self.cnt[name] = 0
        return name

    def ev(self, eng, emit, deps=(), tag=""):
        self.cnt[eng] += 1
        tok = (eng, self.cnt[eng], False)
        self.events[eng].append((list(deps), emit, 1, tag, None))
        return tok

    def dma(self, eng, emit, deps=(), tag="", sem=None):
        ch = eng if sem is None else self.xsem(sem)
        self.cnt[ch] += 16
        tok = (ch, self.cnt[ch], True)
        self.events[eng].append((list(deps), emit, 16, tag, ch))
        return tok

    def cc(self, emit, deps=(), tag=""):
        sem = self.st.enter_context(self.nc.semaphore(f"ccs{self.ncc}"))
        self.ncc += 1

        def emit2(g):
            emit(g).then_inc(sem, 1)
            return None
        self.events["gp"].append((list(deps), emit2, 0, tag, None))
        return ("CC", sem, True)

    def run(self, blk):
        def body(eng_name):
            def _(eng):
                waited = {}
                mysem = self.sems[eng_name]
                for deps, emit, inc, tag, ch in self.events[eng_name]:
                    for d in deps:
                        if d is None:
                            continue
                        if d[0] == "CC":
                            key = id(d[1])
                            if waited.get(key, 0) < 1:
                                eng.wait_ge(d[1], 1)
                                waited[key] = 1
                            continue
                        de, dv_, asyn = d
                        if waited.get(de, 0) < dv_:
                            eng.wait_ge(self.sems[de], dv_)
                            waited[de] = dv_
                    ins = emit(eng)
                    if inc:
                        tsem = mysem if ch is None else self.sems[ch]
                        if ins is None:
                            eng.sem_inc(tsem, inc)
                        else:
                            ins.then_inc(tsem, inc)
            return _
        blk.sync(body("sy"))
        blk.tensor(body("pe"))
        blk.vector(body("dv"))
        blk.scalar(body("ac"))
        blk.gpsimd(body("gp"))


def get_alibi_slopes(n):
    c = 2 ** math.floor(math.log2(n))
    base = 2 ** (-(2 ** -(math.log2(c) - 3)))
    slopes = np.power(base, np.arange(1, 1 + c, dtype=np.float32)).astype(np.float32)
    if c != n:
        eb = 2 ** (-(2 ** -(math.log2(2 * c) - 3)))
        num = min(c, n - c)
        ep = np.arange(1, 1 + 2 * num, 2, dtype=np.float32)
        slopes = np.concatenate([slopes, np.power(eb, ep).astype(np.float32)])
    return slopes


def build():
    nc = bass.Bass("TRN2", target_bir_lowering=False)
    ap = lambda t: t.ap()

    x_slab = ap(nc.dram_tensor("x_slab", [TS, D], F32, kind="ExternalInput"))
    wq_sl = ap(nc.dram_tensor("wq_sl", [D, W3], F32, kind="ExternalInput"))
    bq_sl = ap(nc.dram_tensor("bq_sl", [1, W3], F32, kind="ExternalInput"))
    wo_sl = ap(nc.dram_tensor("wo_sl", [HPC * HD, D], F32, kind="ExternalInput"))
    bout_bc = ap(nc.dram_tensor("bout_bc", [128, D], F32, kind="ExternalInput"))
    c1t = ap(nc.dram_tensor("c1t", [D, RANK], F32, kind="ExternalInput"))
    c2t_sl = ap(nc.dram_tensor("c2t_sl", [RANK, SLICE], F32, kind="ExternalInput"))
    w1_sl = ap(nc.dram_tensor("w1_sl", [SLICE, D], F32, kind="ExternalInput"))
    w2_sl = ap(nc.dram_tensor("w2_sl", [SLICE, D], F32, kind="ExternalInput"))
    alibi_r = ap(nc.dram_tensor("alibi_r", [HPC, 2, S], BF16, kind="ExternalInput"))
    tri_c = ap(nc.dram_tensor("tri_c", [128, 128], F32, kind="ExternalInput"))
    ident_c = ap(nc.dram_tensor("ident_c", [128, 128], BF16, kind="ExternalInput"))
    onesb_c = ap(nc.dram_tensor("onesb_c", [2, 128], BF16, kind="ExternalInput"))

    out_slab = ap(nc.dram_tensor("out_slab", [TS, D], F32, kind="ExternalOutput"))

    dt_ = lambda n_, s_, d_: ap(nc.dram_tensor(n_, s_, d_, kind="Internal"))
    wqb_d = dt_("wqb_d", [D, W3], BF16)
    c1tb_d = dt_("c1tb_d", [D, RANK], BF16)
    w1b_d = dt_("w1b_d", [SLICE, D], BF16)
    w2b_d = dt_("w2b_d", [SLICE, D], BF16)
    xin_sl_d = dt_("xin_sl_d", [TS, D], BF16)
    xin_fu0_d = dt_("xin_fu0_d", [T // 2, D], BF16)
    xin_fu1_d = dt_("xin_fu1_d", [T // 2, D], BF16)
    ctxT_d = dt_("ctxT_d", [HPC * HD, T], BF16)
    att_pt0_d = dt_("att_pt0_d", [T // 2, D], BF16)
    guard0_d = dt_("guard0_d", [T // 2, D], BF16)
    att_pt1_d = dt_("att_pt1_d", [T // 2, D], BF16)
    guard1_d = dt_("guard1_d", [T // 2, D], BF16)
    att_rs0_d = dt_("att_rs0_d", [TS // 2, D], BF16)
    att_rs1_d = dt_("att_rs1_d", [TS // 2, D], BF16)
    xp_d = dt_("xp_d", [TS, D], F32)
    xf_sl_d = dt_("xf_sl_d", [TS, D], BF16)
    xf_fu0_d = dt_("xf_fu0_d", [T // 2, D], BF16)
    xf_fu1_d = dt_("xf_fu1_d", [T // 2, D], BF16)
    r_sl0_d = dt_("r_sl0_d", [RANK, TS // 2], BF16)
    r_sl1_d = dt_("r_sl1_d", [RANK, TS // 2], BF16)
    r_st0_d = dt_("r_st0_d", [NC * RANK, TS // 2], BF16)
    r_st1_d = dt_("r_st1_d", [NC * RANK, TS // 2], BF16)
    ffn_pt0_d = dt_("ffn_pt0_d", [T // 2, D], BF16)
    guard2_d = dt_("guard2_d", [T // 2, D], BF16)
    ffn_pt1_d = dt_("ffn_pt1_d", [T // 2, D], BF16)
    guard3_d = dt_("guard3_d", [T // 2, D], BF16)
    ffn_rs0_d = dt_("ffn_rs0_d", [TS // 2, D], BF16)
    ffn_rs1_d = dt_("ffn_rs1_d", [TS // 2, D], BF16)
    dbg_d = dt_("dbg_d", [TS, D], F32)

    st = ExitStack()
    with st:
        P = Prog(nc, st)
        sb = lambda n_, s_, d_=F32: st.enter_context(nc.sbuf_tensor(n_, s_, d_))

        s_tri = sb("s_tri", [128, 128])
        s_id = sb("s_id", [128, 128], BF16)
        s_onesb = sb("s_onesb", [2, 128], BF16)
        s_alibi = sb("s_alibi", [2, S], BF16)

        p_a = st.enter_context(nc.psum_tensor("p_a", [128, 2048], F32))
        p_b = st.enter_context(nc.psum_tensor("p_b", [128, 1536], F32))
        p_c = st.enter_context(nc.psum_tensor("p_c", [128, 8, 128], BF16))

        e_tri = P.dma("sy", lambda e: e.dma_start(out=s_tri[:, :], in_=tri_c))
        e_id = P.dma("sy", lambda e: e.dma_start(out=s_id[:, :], in_=ident_c))
        e_onb = P.dma("sy", lambda e: e.dma_start(out=s_onesb[:, :], in_=onesb_c))
        e_wqc = P.dma("gp", lambda e: e.dma_start(out=wqb_d, in_=wq_sl))
        e_c1c = P.dma("gp", lambda e: e.dma_start(out=c1tb_d, in_=c1t))

        def emit_ln(loader, storer, n, tag, lst):
            ln_x = lst.enter_context(nc.sbuf_tensor(f"lnx{tag}", [128, 2, D], F32))
            ln_sq = lst.enter_context(nc.sbuf_tensor(f"lnq{tag}", [128, D], BF16))
            ln_st = lst.enter_context(nc.sbuf_tensor(f"lns{tag}", [128, 2, 8], F32))
            ln_o = lst.enter_context(nc.sbuf_tensor(f"lno{tag}", [128, 2, D], BF16))
            out = []
            prev_done = [[], []]
            prev_sq = None
            for i in range(n):
                j = i % 2
                ld = loader(i, ln_x[:, j, :], prev_done[j])
                sqd = [ld] + ([prev_sq] if prev_sq else [])
                sq = P.ev("ac", (lambda j=j: lambda e: e.activation(
                    out=ln_sq[:, :], in_=ln_x[:, j, :], func=AF.Square,
                    accum_out=ln_st[:, j, 1:2]))(), deps=sqd)
                prev_sq = sq
                sm0 = P.ev("dv", (lambda j=j: lambda e: e.reduce_sum(
                    ln_st[:, j, 0:1], ln_x[:, j, :], axis=AX.X))(), deps=[ld])
                sm = P.ev("dv", (lambda j=j: lambda e: e.tensor_scalar(
                    out=ln_st[:, j, 2:3], in0=ln_st[:, j, 0:1],
                    scalar1=1.0 / D, scalar2=None, op0=ALU.mult))(), deps=[sm0])
                var0 = P.ev("dv", (lambda j=j: lambda e: (
                    e.tensor_tensor(out=ln_st[:, j, 6:7], in0=ln_st[:, j, 2:3],
                                    in1=ln_st[:, j, 2:3], op=ALU.mult),
                    e.tensor_scalar(out=ln_st[:, j, 7:8], in0=ln_st[:, j, 1:2],
                                    scalar1=1.0 / D, scalar2=EPS, op0=ALU.mult,
                                    op1=ALU.add),
                )[-1])(), deps=[sq, sm])
                var = P.ev("dv", (lambda j=j: lambda e: e.tensor_tensor(
                    out=ln_st[:, j, 3:4], in0=ln_st[:, j, 7:8],
                    in1=ln_st[:, j, 6:7], op=ALU.subtract))(), deps=[var0])
                rst0 = P.ev("ac", (lambda j=j: lambda e: e.activation(
                    out=ln_st[:, j, 4:5], in_=ln_st[:, j, 3:4],
                    func=AF.Ln, bias=0.0, scale=1.0))(), deps=[var])
                rstd = P.ev("ac", (lambda j=j: lambda e: e.activation(
                    out=ln_st[:, j, 5:6], in_=ln_st[:, j, 4:5],
                    func=AF.Exp, scale=-0.5))(), deps=[rst0])
                xo = P.ev("dv", (lambda j=j: lambda e: e.tensor_scalar(
                    out=ln_o[:, j, :], in0=ln_x[:, j, :],
                    scalar1=ln_st[:, j, 2:3], scalar2=ln_st[:, j, 5:6],
                    op0=ALU.subtract, op1=ALU.mult))(), deps=[rstd])
                toks = storer(i, ln_o[:, j, :], [xo])
                prev_done[j] = toks + [xo]
                out.append((toks, xo))
            return out

        def ld1(i, dst, war):
            return P.dma("sy", (lambda i=i, dst=dst: lambda e: e.dma_start(
                out=dst, in_=x_slab[i * 128:(i + 1) * 128, :]))(), deps=war)

        def st1(i, src, deps):
            return [P.dma("gp", (lambda i=i, src=src: lambda e: e.dma_start(
                out=xin_sl_d[i * 128:(i + 1) * 128, :], in_=src))(), deps=deps,
                sem="xns0" if i < 4 else "xns1")]

        with ExitStack() as lst1:
            ln1 = emit_ln(ld1, st1, NT_SL, "p1", lst1)

        cc_xin0 = P.cc(lambda g: g.collective_compute(
            "AllGather", ALU.bypass, replica_groups=[list(range(NC))],
            ins=[xin_sl_d[0:TS // 2, :]], outs=[xin_fu0_d]),
            deps=[t for tt, _ in ln1[:4] for t in tt], tag="ag_xin0")
        cc_xin1 = P.cc(lambda g: g.collective_compute(
            "AllGather", ALU.bypass, replica_groups=[list(range(NC))],
            ins=[xin_sl_d[TS // 2:TS, :]], outs=[xin_fu1_d]),
            deps=[t for tt, _ in ln1[4:] for t in tt], tag="ag_xin1")

        def finish(last_deps, dbg_src=None, dbg_bf16=True):
            toks = list(last_deps)
            if dbg_src is not None:
                toks = [P.dma("gp", lambda e: e.dma_start(out=dbg_d, in_=dbg_src),
                              deps=toks)]
                toks = [P.dma("sy", lambda e: e.dma_start(out=out_slab, in_=dbg_d),
                              deps=toks)]
            self_tok = P.ev("sy", lambda e: None, deps=toks, tag="final")
            with nc.Block() as blk:
                P.run(blk)

        if STAGE == "xin":
            finish([cc_xin0, cc_xin1], dbg_src=xin_sl_d)
            return nc

        last_pv_of_batch = None
        ctx_store_toks = []

        with (
            nc.sbuf_tensor("wq_sb", [128, 32 * W3], BF16) as wq_sb,
            nc.sbuf_tensor("s_bqb", [1, W3], BF16) as s_bqb,
            nc.sbuf_tensor("xin_s", [128, 32, 512], BF16) as xin_s,
            nc.sbuf_tensor("qkv_sb", [128, 16, W3], BF16) as qkv_sb,
            nc.sbuf_tensor("sdg", [128, 128], F32) as s_diag,
            nc.sbuf_tensor("ssc", [128, S], F32) as s_sc,
            nc.sbuf_tensor("prb", [128, S], BF16) as probs,
            nc.sbuf_tensor("qk_dm", [128, 2, S], BF16) as qk_dm,
            nc.sbuf_tensor("pT", [128, 8, 128], BF16) as pT_sb,
            nc.sbuf_tensor("rdg", [128, 2, 128], BF16) as rd_sb,
            nc.sbuf_tensor("cxe", [128, 2, 128], BF16) as ctx_sb,
            nc.sbuf_tensor("astt", [128, 2, 8], F32) as stat,
        ):
            e_bq = P.dma("gp", lambda e: e.dma_start(out=s_bqb[:, :], in_=bq_sl),
                         deps=[cc_xin0])
            e_wql = P.dma("sy", lambda e: e.dma_start(
                out=wq_sb[:, :].rearrange("p (c n) -> p c n", n=W3),
                in_=wqb_d.rearrange("(c p) n -> p c n", p=128)),
                deps=[e_wqc, cc_xin0])
            prev_scores = None
            prev_exps = None
            prev_evac = None
            prev_sc_user = None
            prev_ctx_evac = None
            prev_mx0 = None
            prev_rdg_user = None
            st_pv = {"cx": None, "last_ptr": None, "last_pv": None,
                     "rdg_user": None}
            prev_subs = [None] * 4
            e_alh = None
            for b in range(B):
                qkv_evac = []
                prev_half_lastmm = None
                for half in range(4):
                    gt = b * S + half * 512
                    loads = []
                    if half % 2 == 0:
                        xsrc0, ccx = xin_fu0_d, cc_xin0
                        ro = gt // 2
                    else:
                        xsrc0, ccx = xin_fu1_d, cc_xin1
                        ro = (gt - 512) // 2
                    for f in range(32):
                        dep = [ccx, e_wql]
                        if prev_subs[f // 8] is not None:
                            dep.append(prev_subs[f // 8])
                        loads.append(P.dma("sy", (lambda f=f, ro=ro,
                                     xsrc0=xsrc0: lambda e:
                                     e.dma_start(out=xin_s[:, f, :],
                                                 in_=xsrc0[ro:ro + 512,
                                                           f * 128:(f + 1) * 128],
                                                 transpose=True))(), deps=dep))
                    sub_last = [None] * 4
                    for tc4 in range(4):
                        gtc = half * 4 + tc4
                        subs = []
                        for k in range(4):
                            deps = loads[k * 8:(k + 1) * 8]
                            if k == 0:
                                deps = deps + [e_onb]
                                if len(qkv_evac) >= 2:
                                    deps.append(qkv_evac[-2])
                                if last_pv_of_batch is not None:
                                    deps.append(last_pv_of_batch)
                                if prev_ctx_evac is not None:
                                    deps.append(prev_ctx_evac)
                                if prev_evac is not None:
                                    deps.append(prev_evac)
                                if prev_mx0 is not None:
                                    deps.append(prev_mx0)
                            else:
                                deps = deps + [subs[-1]]
                            if k == 3:
                                deps = deps + [e_bq]

                            def emit_qs(e, tc4=tc4, gtc=gtc, k=k):
                                pq = p_a if gtc % 2 == 0 else p_b
                                mm = None
                                for f in range(k * 8, (k + 1) * 8):
                                    for o in range(0, W3, 512):
                                        mm = e.matmul(
                                            pq[:, o:o + 512],
                                            lhsT=xin_s[:, f,
                                                       tc4 * 128:
                                                       (tc4 + 1) * 128],
                                            rhs=wq_sb[:, f * W3 + o:
                                                      f * W3 + o + 512],
                                            start=(f == 0), stop=False)
                                if k == 3:
                                    for o in range(0, W3, 512):
                                        mm = e.matmul(
                                            pq[:, o:o + 512],
                                            lhsT=s_onesb[0:1, 0:128],
                                            rhs=s_bqb[:, o:o + 512],
                                            start=False, stop=True)
                                return mm
                            sub = P.ev("pe", emit_qs, deps=deps)
                            subs.append(sub)
                            sub_last[k] = sub
                        mm = subs[-1]
                        prev_half_lastmm = mm
                        ev = P.ev("dv", (lambda gtc=gtc: lambda e: (
                            e.tensor_scalar(out=qkv_sb[:, gtc, 0:HPC * HD],
                                            in0=(p_a if gtc % 2 == 0 else p_b)[:, 0:HPC * HD],
                                            scalar1=SCALE,
                                            scalar2=None, op0=ALU.mult),
                            e.tensor_copy(out=qkv_sb[:, gtc, HPC * HD:W3],
                                          in_=(p_a if gtc % 2 == 0 else p_b)[:, HPC * HD:W3]),
                        )[-1])(), deps=[mm])
                        qkv_evac.append(ev)
                    prev_subs = list(sub_last)

                for h in range(HPC):
                    aldeps = [prev_scores] if prev_scores is not None else []
                    e_alh = P.dma("sy", (lambda h=h: lambda e: e.dma_start(
                        out=s_alibi[:, :], in_=alibi_r[h, :, :]))(), deps=aldeps)
                    tr_copies = []
                    prev_cp = None
                    for tc in range(16):
                        for qk in range(2):
                            col = qk * HPC * HD + h * HD
                            trdeps = [qkv_evac[tc], e_id]
                            if prev_cp is not None:
                                trdeps.append(prev_cp)
                            tr = P.ev("pe", (lambda tc=tc, col=col: lambda e:
                                      e.transpose(p_c[:, 0, :],
                                                  in_=qkv_sb[:, tc, col:col + HD],
                                                  identity=s_id[:, :]))(),
                                      deps=trdeps)
                            cpdeps = [tr]
                            if tc == 0 and qk == 0 and prev_scores is not None:
                                cpdeps.append(prev_scores)
                            cp = P.ev("dv", (lambda tc=tc, qk=qk: lambda e:
                                      e.tensor_copy(
                                          out=qk_dm[:, qk, tc * 128:(tc + 1) * 128],
                                          in_=p_c[:, 0, :]))(), deps=cpdeps)
                            prev_cp = cp
                            tr_copies.append(cp)
                    def emit_pv(pend):
                        qcp, rdiag_tok = pend
                        vcol = 2 * HPC * HD + h * HD
                        nkc = qcp + 1
                        last = {"tr": None, "cp": None, "pv": None}
                        trs = []
                        cps = []
                        pvs = []
                        PVG = int(os.environ.get("MK_PVG", "2"))
                        for g0 in range(0, nkc, PVG):
                            gn = min(PVG, nkc - g0)
                            for i in range(gn):
                                kc = g0 + i
                                trdeps = [rdiag_tok]
                                if kc >= 8:
                                    trdeps.append(cps[kc - 8])
                                tr = P.ev("pe", (lambda kc=kc: lambda e:
                                          e.transpose(
                                              p_c[:, kc % 8, :],
                                              in_=probs[:, kc * 128:(kc + 1) * 128],
                                              identity=s_id[:, :]))(),
                                          deps=trdeps)
                                trs.append(tr)
                            for i in range(gn):
                                kc = g0 + i
                                cpdeps = [trs[kc]]
                                if kc >= 8:
                                    cpdeps.append(pvs[kc - 8])
                                cp = P.ev("dv", (lambda kc=kc: lambda e:
                                            e.tensor_copy(
                                                out=pT_sb[:, kc % 8, :],
                                                in_=p_c[:, kc % 8, :]))(),
                                            deps=cpdeps)
                                cps.append(cp)
                            for i in range(gn):
                                kc = g0 + i
                                pvdeps = [cps[kc]]
                                if kc == 0 and st_pv["cx"] is not None:
                                    pvdeps.append(st_pv["cx"])
                                pv = P.ev("pe", (lambda kc=kc, qcp=qcp, vcol=vcol:
                                          lambda e: e.matmul(
                                              p_b[:, 0:128],
                                              lhsT=qkv_sb[:, kc, vcol:vcol + HD],
                                              rhs=pT_sb[:, kc % 8, :],
                                              start=(kc == 0),
                                              stop=(kc == qcp)))(),
                                          deps=pvdeps)
                                pvs.append(pv)
                        cxdeps = [pvs[-1]]
                        if len(ctx_store_toks) >= 2:
                            cxdeps.append(ctx_store_toks[-2])
                        cx = P.ev("dv", (lambda qcp=qcp: lambda e: e.tensor_copy(
                            out=ctx_sb[:, qcp % 2, :], in_=p_b[:, 0:128]))(),
                            deps=cxdeps)
                        st_pv["cx"] = cx
                        st_pv["last_ptr"] = trs[-1]
                        st_pv["last_pv"] = pvs[-1]
                        st_pv["rdg_user"] = trs[-1]
                        ctx_store_toks.append(P.dma("gp", (
                            lambda h=h, qcp=qcp, b=b: lambda e: e.dma_start(
                                out=ctxT_d[h * HD:(h + 1) * HD,
                                           b * S + qcp * 128:
                                           b * S + (qcp + 1) * 128],
                                in_=ctx_sb[:, qcp % 2, :]))(), deps=[cx]))

                    pending = None
                    for qc in range(NQC):
                        pfx = (qc + 1) * 128
                        so = qc % 2
                        deps = [tr_copies[-1], e_onb, e_alh]
                        if prev_evac is not None:
                            deps.append(prev_evac)
                        if prev_mx0 is not None:
                            deps.append(prev_mx0)

                        def emit_scores(e, qc=qc, pfx=pfx, h=h):
                            o = 0
                            while o < pfx:
                                nn = min(512, pfx - o)
                                e.matmul(p_a[:, o:o + nn],
                                         lhsT=qk_dm[:, 0, qc * 128:(qc + 1) * 128],
                                         rhs=qk_dm[:, 1, o:o + nn],
                                         start=True, stop=False)
                                o += nn
                            o = 0
                            mm = None
                            while o < pfx:
                                nn = min(512, pfx - o)
                                mm = e.matmul(p_a[:, o:o + nn],
                                              lhsT=s_onesb[0:2, 0:128],
                                              rhs=s_alibi[0:2, o:o + nn],
                                              start=False, stop=True)
                                o += nn
                            return mm
                        sc = P.ev("pe", emit_scores, deps=deps)
                        prev_scores = sc
                        dgdeps = [sc, e_tri]
                        if prev_exps is not None:
                            dgdeps.append(prev_exps)
                        dg = P.ev("dv", (lambda qc=qc: lambda e: e.tensor_tensor(
                            out=s_diag[:, :], in0=p_a[:, qc * 128:(qc + 1) * 128],
                            in1=s_tri[:, :], op=ALU.add))(), deps=dgdeps)
                        if qc > 0:
                            sevdeps = [sc]
                            if prev_sc_user is not None:
                                sevdeps.append(prev_sc_user)
                            sev = P.ev("dv", (lambda qc=qc: lambda e:
                                       e.tensor_copy(out=s_sc[:, 0:qc * 128],
                                                     in_=p_a[:, 0:qc * 128]))(),
                                       deps=sevdeps)
                        else:
                            sev = dg
                        prev_evac = sev
                        if pending is not None:
                            emit_pv(pending)
                            pending = None
                        if qc > 0:
                            mx0 = P.ev("dv", (lambda qc=qc, so=so: lambda e: (
                                e.reduce_max(stat[:, so, 0:1], p_a[:, 0:qc * 128],
                                             axis=AX.X, negate=True),
                                e.reduce_max(stat[:, so, 1:2], s_diag[:, :],
                                             axis=AX.X, negate=True),
                            )[-1])(), deps=[dg, sc])
                            mx = P.ev("dv", (lambda so=so: lambda e:
                                      e.tensor_tensor(out=stat[:, so, 2:3],
                                                      in0=stat[:, so, 0:1],
                                                      in1=stat[:, so, 1:2],
                                                      op=ALU.min))(), deps=[mx0])
                        else:
                            mx0 = P.ev("dv", (lambda so=so: lambda e: e.reduce_max(
                                stat[:, so, 2:3], s_diag[:, :], axis=AX.X,
                                negate=True))(), deps=[dg])
                            mx = mx0
                        prev_mx0 = mx0
                        exdeps = [mx]
                        if st_pv["last_ptr"] is not None:
                            exdeps.append(st_pv["last_ptr"])
                        prev_rdg_user = st_pv["rdg_user"]

                        def emit_exps(e, qc=qc, pfx=pfx, so=so):
                            if qc > 0:
                                e.activation(out=probs[:, 0:qc * 128],
                                             in_=s_sc[:, 0:qc * 128], func=AF.Exp,
                                             bias=stat[:, so, 2:3], scale=1.0,
                                             accum_out=stat[:, so, 3:4])
                            return e.activation(out=probs[:, qc * 128:pfx],
                                                in_=s_diag[:, :], func=AF.Exp,
                                                bias=stat[:, so, 2:3], scale=1.0,
                                                accum_out=stat[:, so, 4:5])
                        ex = P.ev("ac", emit_exps, deps=exdeps)
                        prev_exps = ex
                        prev_sc_user = ex

                        def emit_rsum(e, qc=qc, so=so):
                            if qc > 0:
                                return e.tensor_tensor(out=stat[:, so, 5:6],
                                                       in0=stat[:, so, 3:4],
                                                       in1=stat[:, so, 4:5],
                                                       op=ALU.add)
                            return e.tensor_copy(out=stat[:, so, 5:6],
                                                 in_=stat[:, so, 4:5])
                        rsum = P.ev("dv", emit_rsum, deps=[ex])
                        rcp = P.ev("dv", (lambda so=so: lambda e: e.reciprocal(
                            stat[:, so, 6:7], stat[:, so, 5:6]))(), deps=[rsum])
                        psdeps = [rcp]
                        if prev_rdg_user is not None:
                            psdeps.append(prev_rdg_user)
                        pscale = P.ev("dv", (lambda pfx=pfx, so=so: lambda e:
                                     e.tensor_scalar(out=probs[:, 0:pfx],
                                                     in0=probs[:, 0:pfx],
                                                     scalar1=stat[:, so, 6:7],
                                                     scalar2=None,
                                                     op0=ALU.mult))(),
                                     deps=psdeps)
                        pending = (qc, pscale)
                    emit_pv(pending)
                    pending = None
                    prev_ctx_evac = st_pv["cx"]
                    last_pv_of_batch = st_pv["last_pv"]

        e_w1c = P.dma("gp", lambda e: e.dma_start(out=w1b_d, in_=w1_sl),
                      sem="w1cs")
        e_w2c = P.dma("gp", lambda e: e.dma_start(out=w2b_d, in_=w2_sl),
                      sem="w2cs")

        if STAGE == "attn":
            finish([ctx_store_toks[-1]],
                   dbg_src=ctxT_d.rearrange("a (b c) -> (a b) c", c=D))
            return nc

        att_stores = []
        with (
            nc.sbuf_tensor("wo_sb", [128, HPC, D], BF16) as wo_sb,
            nc.sbuf_tensor("cxt", [128, 2, HPC, 128], BF16) as cxt,
            nc.sbuf_tensor("aop", [128, 2, D], BF16) as aop,
        ):
            wo_lds = [P.dma("gp", (lambda hh=hh: lambda e: e.dma_start(
                out=wo_sb[:, hh, :], in_=wo_sl[hh * 128:(hh + 1) * 128, :]))(),
                deps=[ctx_store_toks[-1]])
                for hh in range(HPC)]
            prev_mm = [None, None]
            prev_ev1 = [None, None]
            P4ORD = [tc for ph in (0, 1) for tc in range(NTC)
                     if (tc % 8) // 4 == ph]
            cc_att0 = None
            for kk, tc in enumerate(P4ORD):
                j = kk % 2
                cld = []
                for hh in range(HPC):
                    dep = [ctx_store_toks[-1]]
                    if prev_mm[j] is not None:
                        dep.append(prev_mm[j])
                    cld.append(P.dma("sy", (lambda tc=tc, hh=hh, j=j: lambda e:
                               e.dma_start(out=cxt[:, j, hh, :],
                                           in_=ctxT_d[hh * 128:(hh + 1) * 128,
                                                      tc * 128:(tc + 1) * 128]))(),
                               deps=dep))
                gevs = []
                mm1 = None
                for g in range(8):
                    po = (g % 4) * 512
                    deps = cld + wo_lds
                    if len(gevs) >= 4:
                        deps.append(gevs[g - 4])
                    elif prev_ev1[0] is not None:
                        deps.append(prev_ev1[0][g % 4])

                    def emit_op(e, g=g, j=j, po=po):
                        mm = None
                        for hh in range(HPC):
                            mm = e.matmul(p_a[:, po:po + 512],
                                          lhsT=cxt[:, j, hh, :],
                                          rhs=wo_sb[:, hh,
                                                    g * 512:(g + 1) * 512],
                                          start=(hh == 0),
                                          stop=(hh == HPC - 1))
                        return mm
                    mm1 = P.ev("pe", emit_op, deps=deps)
                    evdeps = [mm1]
                    if len(att_stores) >= 2:
                        evdeps.append(att_stores[-2])
                    gevs.append(P.ev("dv", (lambda g=g, j=j, po=po: lambda e:
                              e.tensor_copy(
                                  out=aop[:, j, g * 512:(g + 1) * 512],
                                  in_=p_a[:, po:po + 512]))(), deps=evdeps))
                ev0, ev1 = gevs[0], gevs[-1]
                prev_mm[j] = mm1
                prev_ev1[0] = gevs[4:8]
                dst = (att_pt0_d[kk * 128:(kk + 1) * 128, :] if kk < 32 else
                       att_pt1_d[(kk - 32) * 128:(kk - 31) * 128, :])
                att_stores.append(P.dma("sy", (lambda dst=dst, j=j: lambda e:
                                  e.dma_start(out=dst,
                                              in_=aop[:, j, :]))(),
                                  deps=[ev0, ev1], sem="ats0" if kk < 32 else "ats1"))
                if kk == 31:
                    cc_att0 = P.cc(lambda g: g.collective_compute(
                        "ReduceScatter", ALU.add,
                        replica_groups=[list(range(NC))],
                        ins=[att_pt0_d], outs=[att_rs0_d]),
                        deps=list(att_stores), tag="rs_att0")

        if STAGE == "outproj":
            finish([att_stores[-1]], dbg_src=att_pt0_d[0:TS, :])
            return nc

        cc_att1 = P.cc(lambda g: g.collective_compute(
            "ReduceScatter", ALU.add, replica_groups=[list(range(NC))],
            ins=[att_pt1_d], outs=[att_rs1_d]), deps=att_stores[32:],
            tag="rs_att1")

        s_c2t = sb("s_c2t", [RANK, SLICE], BF16)
        s_c1tb = sb("s_c1tb", [128, 32 * RANK], BF16)
        s_iotan = sb("s_iotan", [128, SPB])
        s_blkoff = sb("s_blkoff", [128, BPC])
        h_sb = sb("h_sb", [128, BPC * NTC])
        idx1_sb = sb("idx1_sb", [128, BPC * NTC], I32)
        idx1f_sb = sb("idx1f_sb", [128, BPC * NTC])
        idx2_sb = sb("idx2_sb", [128, BPC * NTC], I32)
        e_c2 = P.dma("gp", lambda e: e.dma_start(out=s_c2t[:, :], in_=c2t_sl),
                     deps=[cc_att0, att_stores[-1]])
        e_c1l = P.dma("gp", lambda e: e.dma_start(
            out=s_c1tb[:, :].rearrange("p (c r) -> p c r", r=RANK),
            in_=c1tb_d.rearrange("(c p) r -> p c r", p=128)),
            deps=[e_c1c, cc_att0, att_stores[-1]])
        e_io1 = P.ev("gp", lambda e: e.iota(
            s_iotan[:, :], pattern=[[-1, SPB]], base=SPB - 1, channel_multiplier=0,
            allow_small_or_imprecise_dtypes=True),
            deps=[cc_att0, att_stores[-1]])
        e_io2a = P.ev("gp", lambda e: e.iota(
            s_blkoff[:, :], pattern=[[SPB, BPC]], base=0, channel_multiplier=0,
            allow_small_or_imprecise_dtypes=True),
            deps=[cc_att0, att_stores[-1]])
        e_io2 = P.ev("dv", lambda e: e.tensor_scalar(
            out=s_blkoff[:, :], in0=s_blkoff[:, :], scalar1=float(D),
            scalar2=float((SPB - 1) * D), op0=ALU.mult, op1=ALU.add),
            deps=[e_io2a])

        if STAGE == "attsum":
            d1 = P.dma("gp", lambda e: e.dma_start(
                out=dbg_d[0:TS // 2, :], in_=att_rs0_d), deps=[cc_att0])
            d2 = P.dma("gp", lambda e: e.dma_start(
                out=dbg_d[TS // 2:TS, :], in_=att_rs1_d), deps=[cc_att1])
            finish([d1, d2], dbg_src=dbg_d)
            return nc

        with (
            nc.sbuf_tensor("res_a", [128, 2, D], BF16) as res_a,
            nc.sbuf_tensor("s_bo", [128, D], F32) as s_bo,
            nc.sbuf_tensor("xf_fm", [128, 32, TS], BF16) as xf_fm,
            nc.sbuf_tensor("r_sb", [RANK, TS], BF16) as r_sb,
            ExitStack() as lst2,
        ):
            p4_guard = att_stores[-1]
            e_bo = P.dma("sy", lambda e: e.dma_start(out=s_bo[:, :], in_=bout_bc),
                         deps=[cc_att0, p4_guard])
            tr_last = {}
            all_tr = []
            xf_stores = []

            def ld2(i, dst, war):
                j = i % 2
                lx = P.dma("sy", (lambda i=i, dst=dst: lambda e: e.dma_start(
                    out=dst, in_=x_slab[i * 128:(i + 1) * 128, :]))(),
                    deps=war + [p4_guard])
                rs_src = (att_rs0_d[i * 128:(i + 1) * 128, :] if i < 4 else
                          att_rs1_d[(i - 4) * 128:(i - 3) * 128, :])
                la = P.dma("sy", (lambda src=rs_src, j=j: lambda e:
                    e.dma_start(out=res_a[:, j, :], in_=src))(),
                    deps=[cc_att0 if i < 4 else cc_att1, p4_guard] + war)
                sm0 = P.ev("dv", (lambda j=j, dst=dst: lambda e:
                           e.tensor_tensor(out=dst, in0=dst,
                                           in1=res_a[:, j, :], op=ALU.add))(),
                           deps=[lx, la])
                sm = P.ev("dv", (lambda j=j, dst=dst: lambda e:
                          e.tensor_tensor(out=dst, in0=dst,
                                          in1=s_bo[:, :], op=ALU.add))(),
                          deps=[sm0, e_bo])
                P.dma("sy", (lambda i=i, dst=dst: lambda e: e.dma_start(
                    out=xp_d[i * 128:(i + 1) * 128, :], in_=dst))(), deps=[sm])
                return sm

            def st2(i, src, deps):
                toks = [P.dma("gp", (lambda i=i, src=src: lambda e: e.dma_start(
                    out=xf_sl_d[i * 128:(i + 1) * 128, :], in_=src))(),
                    deps=deps, sem="xfs0" if i < 4 else "xfs1")]
                xf_stores.append(toks[0])
                for ch in range(32):
                    trdeps = list(deps) + [e_id]
                    if "t" in tr_last:
                        trdeps.append(tr_last["t"])
                    tr = P.ev("pe", (lambda ch=ch, src=src: lambda e: e.transpose(
                        p_c[:, 0, :], in_=src[:, ch * 128:(ch + 1) * 128],
                        identity=s_id[:, :]))(), deps=trdeps)
                    cp = P.ev("dv", (lambda ch=ch, i=i: lambda e: e.tensor_copy(
                        out=xf_fm[:, ch, i * 128:(i + 1) * 128],
                        in_=p_c[:, 0, :]))(), deps=[tr])
                    tr_last["t"] = cp
                    all_tr.append(cp)
                toks.append(tr_last["t"])
                return toks

            ln2 = emit_ln(ld2, st2, NT_SL, "p6", lst2)

            def emit_rproj(ph):
                def f(e, ph=ph):
                    mm = None
                    for ch in range(32):
                        mm = e.matmul(p_b[0:RANK, ph * 512:ph * 512 + 512],
                                      lhsT=s_c1tb[:, ch * RANK:(ch + 1) * RANK],
                                      rhs=xf_fm[:, ch,
                                                ph * 512:ph * 512 + 512],
                                      start=(ch == 0), stop=(ch == 31))
                    return mm
                return f
            rmm0 = P.ev("pe", emit_rproj(0),
                        deps=[all_tr[32 * 4 - 1], e_c1l, prev_ctx_evac])
            rev0 = P.ev("dv", lambda e: e.tensor_copy(
                out=r_sb[:, 0:512], in_=p_b[0:RANK, 0:512]), deps=[rmm0])
            rstr0 = P.dma("gp", lambda e: e.dma_start(
                out=r_sl0_d, in_=r_sb[:, 0:512]), deps=[rev0], sem="rss")
            rmm1 = P.ev("pe", emit_rproj(1), deps=[all_tr[-1], e_c1l])
            rev1 = P.ev("dv", lambda e: e.tensor_copy(
                out=r_sb[:, 512:1024], in_=p_b[0:RANK, 512:1024]),
                deps=[rmm1])
            rstr1 = P.dma("gp", lambda e: e.dma_start(
                out=r_sl1_d, in_=r_sb[:, 512:1024]), deps=[rev1], sem="rss")

        cc_xf0 = P.cc(lambda g: g.collective_compute(
            "AllGather", ALU.bypass, replica_groups=[list(range(NC))],
            ins=[xf_sl_d[0:TS // 2, :]], outs=[xf_fu0_d]),
            deps=xf_stores[:4], tag="ag_xf0")
        cc_r0 = P.cc(lambda g: g.collective_compute(
            "AllGather", ALU.bypass, replica_groups=[list(range(NC))],
            ins=[r_sl0_d], outs=[r_st0_d]), deps=[rstr0], tag="ag_r0")
        cc_xf1 = P.cc(lambda g: g.collective_compute(
            "AllGather", ALU.bypass, replica_groups=[list(range(NC))],
            ins=[xf_sl_d[TS // 2:TS, :]], outs=[xf_fu1_d]),
            deps=xf_stores[4:], tag="ag_xf1")
        cc_r1 = P.cc(lambda g: g.collective_compute(
            "AllGather", ALU.bypass, replica_groups=[list(range(NC))],
            ins=[r_sl1_d], outs=[r_st1_d]), deps=[rstr1], tag="ag_r1")

        if STAGE == "xf":
            finish([cc_xf0, cc_xf1, cc_r0, cc_r1], dbg_src=xf_sl_d)
            return nc

        with (
            nc.sbuf_tensor("r_t", [RANK, 4, 128], BF16) as r_t,
            nc.sbuf_tensor("amx", [128, BPC], F32) as amx,
            nc.sbuf_tensor("amall", [128, SLICE], BF16) as amall,
            nc.sbuf_tensor("amt", [128, BPC * 4], F32) as amt,
            nc.sbuf_tensor("xf_t", [128, 2, D], BF16) as xf_t,
            nc.sbuf_tensor("g1", [128, 7, D], BF16) as g1,
            nc.sbuf_tensor("pmul", [128, 2, D], BF16) as pmul,
            nc.sbuf_tensor("hacc", [128, 2, BPC], F32) as hacc,
            nc.sbuf_tensor("g2", [128, BPC, D], BF16) as g2,
            nc.sbuf_tensor("hd", [128, 2 * BPC, 128], BF16) as hd,
            nc.sbuf_tensor("fo", [128, D], BF16) as fo,
        ):
            W2G = [(o, 512) for o in range(0, D, 512)]
            TORD = [tt for ph in (0, 1) for tt in range(NTC)
                    if (tt % 8) // 4 == ph]
            am_toks = [None] * (NTC + 1)
            a2_list = [None] * (NTC + 1)
            imm_list = [None] * (NTC + 2)
            rl_list = [None] * (NTC + 4)
            xl_list = [None] * (NTC + 2)
            h_toks = [None] * NTC
            mul_list = []
            red_list = []
            gt1s = {}
            g2_list = [None] * NTC
            hd_list = [None] * NTC
            wmm_list = [None] * NTC
            ev_list = [None] * NTC
            st_list = [None] * NTC
            ffn_stores = []

            def emit_rl(t):
                if t >= NTC:
                    return
                tl = TORD[t]
                dep = [cc_r1 if t < 6 else (cc_r0 if t < 32 else cc_r1)]
                if t >= 4 and imm_list[t - 4] is not None:
                    dep.append(imm_list[t - 4])
                rsrc = (r_st0_d[RANK * (tl // 8):RANK * (tl // 8) + RANK,
                                128 * (tl % 8):128 * (tl % 8) + 128]
                        if t < 32 else
                        r_st1_d[RANK * (tl // 8):RANK * (tl // 8) + RANK,
                                128 * (tl % 8 - 4):128 * (tl % 8 - 4) + 128])
                rl_list[t] = P.dma("sy", (lambda t=t, rsrc=rsrc: lambda e:
                    e.dma_start(out=r_t[:, t % 4, :], in_=rsrc))(),
                    deps=dep, sem="rls")

            def emit_imm(t):
                if t >= NTC:
                    return
                deps = [rl_list[t], e_c2]
                if t >= 1:
                    deps.append(a2_list[t - 1])

                def f(e, j=t % 4):
                    mm = None
                    for o in range(0, SLICE, 512):
                        nn = min(512, SLICE - o)
                        mm = e.matmul(p_a[:, o:o + nn], lhsT=r_t[:, j, :],
                                      rhs=s_c2t[:, o:o + nn], start=True,
                                      stop=True)
                    return mm
                imm_list[t] = P.ev("pe", f, deps=deps)

            def emit_xl(t):
                if t >= NTC:
                    return
                tl = TORD[t]
                dep = [cc_r1 if t < 4 else (cc_xf0 if t < 32 else cc_xf1)]
                if t >= 2:
                    dep.append(mul_list[(t - 1) * BPC - 1])
                ro = ((tl // 8) * 512 + (tl % 8) * 128 if t < 32 else
                      (tl // 8) * 512 + (tl % 8 - 4) * 128)
                xsrc = (xf_fu0_d[ro:ro + 128, :] if t < 32 else
                        xf_fu1_d[ro:ro + 128, :])
                xl_list[t] = P.dma("sy", (lambda t=t, xsrc=xsrc: lambda e:
                    e.dma_start(out=xf_t[:, t % 2, :], in_=xsrc))(), deps=dep,
                    sem="xls")

            def emit_argmax(t):
                if t >= NTC:
                    return
                a1 = P.ev("dv", lambda e: e.reduce_max(
                    amx[:, :], p_a[:, 0:SLICE].rearrange("p (b s) -> p b s",
                                                         s=SPB), axis=AX.X),
                    deps=[imm_list[t]])

                def emit_eq(e):
                    mm = None
                    for bb in range(BPC):
                        mm = e.scalar_tensor_tensor(
                            out=amall[:, bb * SPB:(bb + 1) * SPB],
                            in0=p_a[:, bb * SPB:(bb + 1) * SPB],
                            scalar=amx[:, bb:bb + 1], in1=s_iotan[:, :],
                            op0=ALU.is_equal, op1=ALU.mult)
                    return mm
                a2 = P.ev("dv", emit_eq, deps=[a1, e_io1])
                a2_list[t] = a2
                a3 = P.ev("dv", lambda e: e.reduce_max(
                    amt[:, 0:BPC],
                    amall[:, :].rearrange("p (b s) -> p b s", s=SPB),
                    axis=AX.X), deps=[a2])
                a6 = P.ev("dv", (lambda t=t: lambda e:
                    e.scalar_tensor_tensor(
                        out=idx1f_sb[:, t * BPC:(t + 1) * BPC],
                        in0=amt[:, 0:BPC], scalar=-float(D),
                        in1=s_blkoff[:, :],
                        op0=ALU.mult, op1=ALU.add))(), deps=[a3, e_io2])
                am_toks[t] = P.ev("dv", (lambda t=t: lambda e: e.tensor_copy(
                    out=idx1_sb[:, t * BPC:(t + 1) * BPC],
                    in_=idx1f_sb[:, t * BPC:(t + 1) * BPC]))(), deps=[a6])

            def emit_g1(t, bb, war):
                if t >= NTC:
                    return
                pi = t * BPC + bb
                agd = [cc_r1] if t < 4 else []
                gt1s[(t, bb)] = P.dma("gp", (lambda t=t, bb=bb, j2=pi % 7:
                           lambda e: e.indirect_dma_start(
                               out=g1[:, j2, :], out_offset=None,
                               in_=w1b_d.rearrange("(x p) d -> x (p d)",
                                                   x=128),
                               in_offset=IndirectOffsetOnAxis(
                                   idx1_sb[:, t * BPC + bb:t * BPC + bb + 1],
                                   axis=1),
                               bounds_check=None))(),
                           deps=[am_toks[t], e_w1c] + war + agd, sem="g1s")

            def emit_g2bb(t, bb):
                if t < 0:
                    return
                gdeps2 = [am_toks[t], e_w2c]
                if t < 4:
                    gdeps2.append(cc_r1)
                if t >= 1 and wmm_list[t - 1] is not None:
                    gdeps2.append(wmm_list[t - 1])
                if g2_list[t] is None:
                    g2_list[t] = []
                g2_list[t].append(P.dma("gp", (lambda t=t, bb=bb: lambda e:
                           e.indirect_dma_start(
                               out=g2[:, bb, :],
                               out_offset=None,
                               in_=w2b_d.rearrange("(x p) d -> x (p d)",
                                                   x=128),
                               in_offset=IndirectOffsetOnAxis(
                                   idx1_sb[:, t * BPC + bb:
                                           t * BPC + bb + 1],
                                   axis=1),
                               bounds_check=None))(), deps=gdeps2,
                           sem="g2s"))

            def emit_hd(t):
                hds = []
                for bb in range(BPC):
                    hdd = [h_toks[t], e_id]
                    if t >= 2 and wmm_list[t - 2] is not None:
                        hdd.append(wmm_list[t - 2])
                    hds.append(P.ev("dv", (lambda t1=t, bb=bb: lambda e:
                               e.tensor_scalar(
                                   out=hd[:, (t1 % 2) * BPC + bb, :],
                                   in0=s_id[:, :],
                                   scalar1=h_sb[:, t1 * BPC + bb:
                                                t1 * BPC + bb + 1],
                                   scalar2=None, op0=ALU.mult))(),
                               deps=hdd))
                hd_list[t] = hds

            def emit_w2grp(t, gi, evs):
                go, gw = W2G[gi]
                po = (gi % 2) * 512
                deps2 = g2_list[t] + hd_list[t]
                if gi >= 2:
                    deps2.append(evs[gi - 2])
                elif t >= 1 and ev_list[t - 1] is not None:
                    deps2.append(ev_list[t - 1][6 + gi])

                def f(e, go=go, gw=gw, t=t, po=po):
                    mm = None
                    for bb in range(BPC):
                        s2 = (t % 2) * BPC + bb
                        mm = e.matmul(p_b[:, po:po + gw],
                                      lhsT=hd[:, s2, :],
                                      rhs=g2[:, bb, go:go + gw],
                                      start=(bb == 0),
                                      stop=(bb == BPC - 1))
                    return mm
                wmm = P.ev("pe", f, deps=deps2)
                evdeps = [wmm]
                if t >= 1 and st_list[t - 1] is not None:
                    evdeps.append(st_list[t - 1])
                gev = P.ev("dv", (lambda go=go, gw=gw, po=po:
                           lambda e: e.tensor_copy(
                               out=fo[:, go:go + gw],
                               in_=p_b[:, po:po + gw]))(), deps=evdeps)
                evs.append(gev)
                return wmm

            for k in range(3):
                emit_rl(k)
            emit_imm(0)
            emit_xl(0)
            emit_xl(1)
            emit_argmax(0)
            emit_rl(3)
            emit_imm(1)
            for k in range(BPC):
                emit_g1(0, k, [])

            cc_ffn0_box = [None]

            def emit_w2tail(t1):
                evs = []
                wmm = None
                for gi in range(8):
                    wmm = emit_w2grp(t1, gi, evs)
                wmm_list[t1] = wmm
                ev_list[t1] = evs
                sdst = (ffn_pt0_d[t1 * 128:(t1 + 1) * 128, :] if t1 < 32
                        else ffn_pt1_d[(t1 - 32) * 128:(t1 - 31) * 128, :])
                st_list[t1] = P.dma("sy", (lambda sdst=sdst: lambda e:
                                 e.dma_start(out=sdst,
                                             in_=fo[:, :]))(), deps=evs,
                                 sem="sts0" if t1 < 32 else "sts1")
                ffn_stores.append(st_list[t1])

            for t in range(NTC):
                if t >= 1:
                    emit_hd(t - 1)
                emit_argmax(t + 1)
                emit_rl(t + 4)
                emit_imm(t + 2)
                reds = []
                for bb in range(BPC):
                    pi = t * BPC + bb
                    mldeps = [gt1s[(t, bb)], xl_list[t]]
                    if pi >= 2:
                        mldeps.append(red_list[pi - 2])
                    ml = P.ev("dv", (lambda j2=pi % 7, jp=pi % 2, j=t % 2:
                              lambda e:
                              e.tensor_mul(pmul[:, jp, :], g1[:, j2, :],
                                           xf_t[:, j, :]))(), deps=mldeps)
                    mul_list.append(ml)
                    rddeps = [ml]
                    if t >= 2:
                        rddeps.append(h_toks[t - 2])
                    rd = P.ev("ac", (lambda bb=bb, j2=pi % 2, hj=t % 2:
                              lambda e: e.activation(
                                  out=pmul[:, j2, :], in_=pmul[:, j2, :],
                                  func=AF.Copy,
                                  accum_out=hacc[:, hj, bb:bb + 1]))(),
                              deps=rddeps)
                    red_list.append(rd)
                    reds.append(rd)
                    if t >= 1:
                        emit_g2bb(t - 1, bb)
                hf = P.ev("dv", (lambda t=t, hj=t % 2: lambda e:
                    e.tensor_scalar(
                        out=h_sb[:, t * BPC:(t + 1) * BPC],
                        in0=hacc[:, hj, :], scalar1=0.0, scalar2=None,
                        op0=ALU.max))(), deps=reds)
                h_toks[t] = hf
                if t >= 1:
                    emit_w2tail(t - 1)
                emit_xl(t + 2)
                for k in range(BPC):
                    emit_g1(t + 1, k, [mul_list[t * BPC + k]])


            t = NTC
            emit_hd(NTC - 1)
            for bb in range(BPC):
                emit_g2bb(NTC - 1, bb)
            cc_ffn0_box[0] = P.cc(lambda g: g.collective_compute(
                "ReduceScatter", ALU.add,
                replica_groups=[list(range(NC))],
                ins=[ffn_pt0_d], outs=[ffn_rs0_d]),
                deps=st_list[0:32], tag="rs_ffn0")
            emit_w2tail(NTC - 1)
            am_toks = [x for x in am_toks if x is not None]
            h_toks = [x for x in h_toks if x is not None]

        if STAGE == "ffnp":
            finish([ffn_stores[-1]], dbg_src=ffn_pt0_d[0:TS, :])
            return nc
        if STAGE == "am":
            t1 = P.dma("gp", lambda e: e.dma_start(
                out=dbg_d[0:128, 0:BPC * NTC], in_=idx1f_sb[:, :]),
                deps=am_toks + h_toks)
            t2 = P.dma("gp", lambda e: e.dma_start(
                out=dbg_d[128:256, 0:BPC * NTC], in_=h_sb[:, :]),
                deps=am_toks + h_toks)
            t3 = P.dma("sy", lambda e: e.dma_start(out=out_slab, in_=dbg_d),
                       deps=[t1, t2])
            finish([t3])
            return nc

        cc_ffn0 = cc_ffn0_box[0]
        cc_ffn1 = P.cc(lambda g: g.collective_compute(
            "ReduceScatter", ALU.add, replica_groups=[list(range(NC))],
            ins=[ffn_pt1_d], outs=[ffn_rs1_d]), deps=ffn_stores[32:],
            tag="rs_ffn1")

        with (
            nc.sbuf_tensor("f_x", [128, 2, D], F32) as f_x,
            nc.sbuf_tensor("f_f", [128, 2, D], BF16) as f_f,
            nc.sbuf_tensor("f_o", [128, 2, D], F32) as f_o,
        ):
            fin_toks = []
            prev_add = [None, None]
            for i in range(NT_SL):
                j = i % 2
                war = [prev_add[j]] if prev_add[j] else []
                ccd = cc_ffn0 if i < 4 else cc_ffn1
                fsrc = (ffn_rs0_d[i * 128:(i + 1) * 128, :] if i < 4 else
                        ffn_rs1_d[(i - 4) * 128:(i - 3) * 128, :])
                lx = P.dma("sy", (lambda i=i, j=j: lambda e: e.dma_start(
                    out=f_x[:, j, :],
                    in_=xp_d[i * 128:(i + 1) * 128, :]))(), deps=war + [ccd])
                lf = P.dma("sy", (lambda fsrc=fsrc, j=j: lambda e:
                    e.dma_start(out=f_f[:, j, :], in_=fsrc))(),
                    deps=[ccd] + war)
                ad = P.ev("dv", (lambda j=j: lambda e: e.tensor_tensor(
                    out=f_o[:, j, :], in0=f_x[:, j, :], in1=f_f[:, j, :],
                    op=ALU.add))(), deps=[lx, lf])
                prev_add[j] = ad
                fin_toks.append(P.dma("sy", (lambda i=i, j=j: lambda e:
                                e.dma_start(
                                    out=out_slab[i * 128:(i + 1) * 128, :],
                                    in_=f_o[:, j, :]))(), deps=[ad]))

        finish(fin_toks)
        return nc

    return nc


_NC_CACHE = {}


def kernel(L0_attn_hidden_states, L0_attn_w_qkv, L0_attn_b_qkv, L0_attn_w_out,
           L0_attn_b_out, L0_ffn_c1_w, L0_ffn_c2_w, L0_ffn_w1, L0_ffn_w2t):
    x = np.ascontiguousarray(np.asarray(L0_attn_hidden_states,
                                        dtype=np.float32)).reshape(T, D)
    wq = np.asarray(L0_attn_w_qkv, dtype=np.float32)
    bq = np.asarray(L0_attn_b_qkv, dtype=np.float32)
    wo = np.asarray(L0_attn_w_out, dtype=np.float32)
    bo = np.asarray(L0_attn_b_out, dtype=np.float32)
    c1 = np.asarray(L0_ffn_c1_w, dtype=np.float32)
    c2 = np.asarray(L0_ffn_c2_w, dtype=np.float32)
    w1 = np.asarray(L0_ffn_w1, dtype=np.float32)
    w2 = np.asarray(L0_ffn_w2t, dtype=np.float32)

    slopes = get_alibi_slopes(H)
    pos = np.arange(S, dtype=np.float32)
    alibi = slopes[:, None] * pos[None, :]
    al_hi = alibi.astype(ml_dtypes.bfloat16)
    al_lo = (alibi - al_hi.astype(np.float32)).astype(ml_dtypes.bfloat16)
    al_hl = np.stack([al_hi, al_lo], axis=1)
    tri = np.where(np.arange(128)[None, :] > np.arange(128)[:, None],
                   np.float32(-1.0e9), np.float32(0.0)).astype(np.float32)
    ident = np.eye(128, dtype=ml_dtypes.bfloat16)
    onesb = np.ones((2, 128), dtype=ml_dtypes.bfloat16)
    bout = np.ascontiguousarray(np.broadcast_to(bo[None, :], (128, D)))
    c1t_host = np.ascontiguousarray(c1.T)

    in_maps = []
    for c in range(NC):
        hs = slice(c * HPC * HD, (c + 1) * HPC * HD)
        wq_c = np.ascontiguousarray(
            np.concatenate([wq[:, 0 * D:][:, hs], wq[:, 1 * D:][:, hs],
                            wq[:, 2 * D:][:, hs]], axis=1))
        bq_c = np.ascontiguousarray(
            np.concatenate([bq[0 * D:1 * D][hs], bq[1 * D:2 * D][hs],
                            bq[2 * D:3 * D][hs]])[None, :])
        rs = slice(c * SLICE, (c + 1) * SLICE)
        al_c = np.ascontiguousarray(al_hl[c * HPC:(c + 1) * HPC])
        in_maps.append({
            "x_slab": np.ascontiguousarray(x[c * TS:(c + 1) * TS]),
            "wq_sl": wq_c, "bq_sl": bq_c,
            "wo_sl": np.ascontiguousarray(wo[c * HPC * HD:(c + 1) * HPC * HD]),
            "bout_bc": bout,
            "c1t": c1t_host,
            "c2t_sl": np.ascontiguousarray(c2[rs].T),
            "w1_sl": np.ascontiguousarray(w1[rs]),
            "w2_sl": np.ascontiguousarray(w2[rs]),
            "alibi_r": al_c, "tri_c": tri, "ident_c": ident,
            "onesb_c": onesb,
        })

    key = STAGE
    if key not in _NC_CACHE:
        _NC_CACHE[key] = build()
    res = run_bass_kernel_spmd(_NC_CACHE[key], in_maps, core_ids=list(range(NC)))
    out = np.concatenate([np.asarray(r["out_slab"]) for r in res.results],
                         axis=0)
    return out.reshape(B, S, D).astype(np.float32)


if __name__ == "__main__":
    nc = build()
    print("build ok; instructions:",
          {k: len(v) for k, v in
           {e: [1] * 0 for e in ENGS}.items()} if False else "n/a")


# version 26
# speedup vs baseline: 1.0071x; 1.0012x over previous
import math
import os
import numpy as np
import ml_dtypes
import concourse.bass as bass
import concourse.mybir as mybir
from concourse.bass_utils import run_bass_kernel_spmd
from concourse.bass import IndirectOffsetOnAxis
from contextlib import ExitStack

F32 = mybir.dt.float32
BF16 = mybir.dt.bfloat16
I32 = mybir.dt.int32
AF = mybir.ActivationFunctionType
ALU = mybir.AluOpType
AX = mybir.AxisListType

NC = 8
B, S, D = 4, 2048, 4096
T = B * S
TS = T // NC
H, HD = 32, 128
HPC = H // NC
W3 = 3 * HPC * HD
INTER, RANK, SPB = 14336, 16, 256
NBLK = INTER // SPB
BPC = NBLK // NC
SLICE = BPC * SPB
SCALE = HD ** -0.5
EPS = 1e-5
BIGIDX = float(2 ** 23)
NQC = S // 128
NTC = T // 128
NT_SL = TS // 128
STAGE = os.environ.get("MK_STAGE", "full")

ENGS = ("sy", "pe", "dv", "ac", "gp")


class Prog:
    def __init__(self, nc, st):
        self.nc = nc
        self.events = {e: [] for e in ENGS}
        self.cnt = {e: 0 for e in ENGS}
        self.sems = {e: st.enter_context(nc.semaphore(f"sem_{e}")) for e in ENGS}
        self.ncc = 0
        self.st = st

    def xsem(self, name):
        if name not in self.sems:
            self.sems[name] = self.st.enter_context(
                self.nc.semaphore(f"sem_{name}"))
            self.cnt[name] = 0
        return name

    def ev(self, eng, emit, deps=(), tag=""):
        self.cnt[eng] += 1
        tok = (eng, self.cnt[eng], False)
        self.events[eng].append((list(deps), emit, 1, tag, None))
        return tok

    def dma(self, eng, emit, deps=(), tag="", sem=None):
        ch = eng if sem is None else self.xsem(sem)
        self.cnt[ch] += 16
        tok = (ch, self.cnt[ch], True)
        self.events[eng].append((list(deps), emit, 16, tag, ch))
        return tok

    def cc(self, emit, deps=(), tag=""):
        sem = self.st.enter_context(self.nc.semaphore(f"ccs{self.ncc}"))
        self.ncc += 1

        def emit2(g):
            emit(g).then_inc(sem, 1)
            return None
        self.events["gp"].append((list(deps), emit2, 0, tag, None))
        return ("CC", sem, True)

    def run(self, blk):
        def body(eng_name):
            def _(eng):
                waited = {}
                mysem = self.sems[eng_name]
                for deps, emit, inc, tag, ch in self.events[eng_name]:
                    for d in deps:
                        if d is None:
                            continue
                        if d[0] == "CC":
                            key = id(d[1])
                            if waited.get(key, 0) < 1:
                                eng.wait_ge(d[1], 1)
                                waited[key] = 1
                            continue
                        de, dv_, asyn = d
                        if waited.get(de, 0) < dv_:
                            eng.wait_ge(self.sems[de], dv_)
                            waited[de] = dv_
                    ins = emit(eng)
                    if inc:
                        tsem = mysem if ch is None else self.sems[ch]
                        if ins is None:
                            eng.sem_inc(tsem, inc)
                        else:
                            ins.then_inc(tsem, inc)
            return _
        blk.sync(body("sy"))
        blk.tensor(body("pe"))
        blk.vector(body("dv"))
        blk.scalar(body("ac"))
        blk.gpsimd(body("gp"))


def get_alibi_slopes(n):
    c = 2 ** math.floor(math.log2(n))
    base = 2 ** (-(2 ** -(math.log2(c) - 3)))
    slopes = np.power(base, np.arange(1, 1 + c, dtype=np.float32)).astype(np.float32)
    if c != n:
        eb = 2 ** (-(2 ** -(math.log2(2 * c) - 3)))
        num = min(c, n - c)
        ep = np.arange(1, 1 + 2 * num, 2, dtype=np.float32)
        slopes = np.concatenate([slopes, np.power(eb, ep).astype(np.float32)])
    return slopes


def build():
    nc = bass.Bass("TRN2", target_bir_lowering=False)
    ap = lambda t: t.ap()

    x_slab = ap(nc.dram_tensor("x_slab", [TS, D], F32, kind="ExternalInput"))
    wq_sl = ap(nc.dram_tensor("wq_sl", [D, W3], F32, kind="ExternalInput"))
    bq_sl = ap(nc.dram_tensor("bq_sl", [1, W3], F32, kind="ExternalInput"))
    wo_sl = ap(nc.dram_tensor("wo_sl", [HPC * HD, D], F32, kind="ExternalInput"))
    bout_bc = ap(nc.dram_tensor("bout_bc", [128, D], F32, kind="ExternalInput"))
    c1t = ap(nc.dram_tensor("c1t", [D, RANK], F32, kind="ExternalInput"))
    c2t_sl = ap(nc.dram_tensor("c2t_sl", [RANK, SLICE], F32, kind="ExternalInput"))
    w1_sl = ap(nc.dram_tensor("w1_sl", [SLICE, D], F32, kind="ExternalInput"))
    w2_sl = ap(nc.dram_tensor("w2_sl", [SLICE, D], F32, kind="ExternalInput"))
    alibi_r = ap(nc.dram_tensor("alibi_r", [HPC, 2, S], BF16, kind="ExternalInput"))
    tri_c = ap(nc.dram_tensor("tri_c", [128, 128], F32, kind="ExternalInput"))
    ident_c = ap(nc.dram_tensor("ident_c", [128, 128], BF16, kind="ExternalInput"))
    onesb_c = ap(nc.dram_tensor("onesb_c", [2, 128], BF16, kind="ExternalInput"))

    out_slab = ap(nc.dram_tensor("out_slab", [TS, D], F32, kind="ExternalOutput"))

    dt_ = lambda n_, s_, d_: ap(nc.dram_tensor(n_, s_, d_, kind="Internal"))
    wqb_d = dt_("wqb_d", [D, W3], BF16)
    c1tb_d = dt_("c1tb_d", [D, RANK], BF16)
    w1b_d = dt_("w1b_d", [SLICE, D], BF16)
    w2b_d = dt_("w2b_d", [SLICE, D], BF16)
    xin_sl_d = dt_("xin_sl_d", [TS, D], BF16)
    xin_fu0_d = dt_("xin_fu0_d", [T // 2, D], BF16)
    xin_fu1_d = dt_("xin_fu1_d", [T // 2, D], BF16)
    ctxT_d = dt_("ctxT_d", [HPC * HD, T], BF16)
    att_pt0_d = dt_("att_pt0_d", [T // 2, D], BF16)
    guard0_d = dt_("guard0_d", [T // 2, D], BF16)
    att_pt1_d = dt_("att_pt1_d", [T // 2, D], BF16)
    guard1_d = dt_("guard1_d", [T // 2, D], BF16)
    att_rs0_d = dt_("att_rs0_d", [TS // 2, D], BF16)
    att_rs1_d = dt_("att_rs1_d", [TS // 2, D], BF16)
    xp_d = dt_("xp_d", [TS, D], F32)
    xf_sl_d = dt_("xf_sl_d", [TS, D], BF16)
    xf_fu0_d = dt_("xf_fu0_d", [T // 2, D], BF16)
    xf_fu1_d = dt_("xf_fu1_d", [T // 2, D], BF16)
    r_sl0_d = dt_("r_sl0_d", [RANK, TS // 2], BF16)
    r_sl1_d = dt_("r_sl1_d", [RANK, TS // 2], BF16)
    r_st0_d = dt_("r_st0_d", [NC * RANK, TS // 2], BF16)
    r_st1_d = dt_("r_st1_d", [NC * RANK, TS // 2], BF16)
    ffn_pt0_d = dt_("ffn_pt0_d", [T // 2, D], BF16)
    guard2_d = dt_("guard2_d", [T // 2, D], BF16)
    ffn_pt1_d = dt_("ffn_pt1_d", [T // 2, D], BF16)
    guard3_d = dt_("guard3_d", [T // 2, D], BF16)
    ffn_rs0_d = dt_("ffn_rs0_d", [TS // 2, D], BF16)
    ffn_rs1_d = dt_("ffn_rs1_d", [TS // 2, D], BF16)
    dbg_d = dt_("dbg_d", [TS, D], F32)

    st = ExitStack()
    with st:
        P = Prog(nc, st)
        sb = lambda n_, s_, d_=F32: st.enter_context(nc.sbuf_tensor(n_, s_, d_))

        s_tri = sb("s_tri", [128, 128])
        s_id = sb("s_id", [128, 128], BF16)
        s_onesb = sb("s_onesb", [2, 128], BF16)
        s_alibi = sb("s_alibi", [2, S], BF16)

        p_a = st.enter_context(nc.psum_tensor("p_a", [128, 2048], F32))
        p_b = st.enter_context(nc.psum_tensor("p_b", [128, 1536], F32))
        p_c = st.enter_context(nc.psum_tensor("p_c", [128, 8, 128], BF16))

        e_tri = P.dma("sy", lambda e: e.dma_start(out=s_tri[:, :], in_=tri_c))
        e_id = P.dma("sy", lambda e: e.dma_start(out=s_id[:, :], in_=ident_c))
        e_onb = P.dma("sy", lambda e: e.dma_start(out=s_onesb[:, :], in_=onesb_c))
        e_wqc = P.dma("gp", lambda e: e.dma_start(out=wqb_d, in_=wq_sl))
        e_c1c = P.dma("gp", lambda e: e.dma_start(out=c1tb_d, in_=c1t))

        def emit_ln(loader, storer, n, tag, lst):
            ln_x = lst.enter_context(nc.sbuf_tensor(f"lnx{tag}", [128, 2, D], F32))
            ln_sq = lst.enter_context(nc.sbuf_tensor(f"lnq{tag}", [128, D], BF16))
            ln_st = lst.enter_context(nc.sbuf_tensor(f"lns{tag}", [128, 2, 8], F32))
            ln_o = lst.enter_context(nc.sbuf_tensor(f"lno{tag}", [128, 2, D], BF16))
            out = []
            prev_done = [[], []]
            prev_sq = None
            for i in range(n):
                j = i % 2
                ld = loader(i, ln_x[:, j, :], prev_done[j])
                sqd = [ld] + ([prev_sq] if prev_sq else [])
                sq = P.ev("ac", (lambda j=j: lambda e: e.activation(
                    out=ln_sq[:, :], in_=ln_x[:, j, :], func=AF.Square,
                    accum_out=ln_st[:, j, 1:2]))(), deps=sqd)
                prev_sq = sq
                sm0 = P.ev("dv", (lambda j=j: lambda e: e.reduce_sum(
                    ln_st[:, j, 0:1], ln_x[:, j, :], axis=AX.X))(), deps=[ld])
                sm = P.ev("dv", (lambda j=j: lambda e: e.tensor_scalar(
                    out=ln_st[:, j, 2:3], in0=ln_st[:, j, 0:1],
                    scalar1=1.0 / D, scalar2=None, op0=ALU.mult))(), deps=[sm0])
                var0 = P.ev("dv", (lambda j=j: lambda e: (
                    e.tensor_tensor(out=ln_st[:, j, 6:7], in0=ln_st[:, j, 2:3],
                                    in1=ln_st[:, j, 2:3], op=ALU.mult),
                    e.tensor_scalar(out=ln_st[:, j, 7:8], in0=ln_st[:, j, 1:2],
                                    scalar1=1.0 / D, scalar2=EPS, op0=ALU.mult,
                                    op1=ALU.add),
                )[-1])(), deps=[sq, sm])
                var = P.ev("dv", (lambda j=j: lambda e: e.tensor_tensor(
                    out=ln_st[:, j, 3:4], in0=ln_st[:, j, 7:8],
                    in1=ln_st[:, j, 6:7], op=ALU.subtract))(), deps=[var0])
                rst0 = P.ev("ac", (lambda j=j: lambda e: e.activation(
                    out=ln_st[:, j, 4:5], in_=ln_st[:, j, 3:4],
                    func=AF.Ln, bias=0.0, scale=1.0))(), deps=[var])
                rstd = P.ev("ac", (lambda j=j: lambda e: e.activation(
                    out=ln_st[:, j, 5:6], in_=ln_st[:, j, 4:5],
                    func=AF.Exp, scale=-0.5))(), deps=[rst0])
                xo = P.ev("dv", (lambda j=j: lambda e: e.tensor_scalar(
                    out=ln_o[:, j, :], in0=ln_x[:, j, :],
                    scalar1=ln_st[:, j, 2:3], scalar2=ln_st[:, j, 5:6],
                    op0=ALU.subtract, op1=ALU.mult))(), deps=[rstd])
                toks = storer(i, ln_o[:, j, :], [xo])
                prev_done[j] = toks + [xo]
                out.append((toks, xo))
            return out

        def ld1(i, dst, war):
            return P.dma("sy", (lambda i=i, dst=dst: lambda e: e.dma_start(
                out=dst, in_=x_slab[i * 128:(i + 1) * 128, :]))(), deps=war)

        def st1(i, src, deps):
            return [P.dma("gp", (lambda i=i, src=src: lambda e: e.dma_start(
                out=xin_sl_d[i * 128:(i + 1) * 128, :], in_=src))(), deps=deps,
                sem="xns0" if i < 4 else "xns1")]

        with ExitStack() as lst1:
            ln1 = emit_ln(ld1, st1, NT_SL, "p1", lst1)

        cc_xin0 = P.cc(lambda g: g.collective_compute(
            "AllGather", ALU.bypass, replica_groups=[list(range(NC))],
            ins=[xin_sl_d[0:TS // 2, :]], outs=[xin_fu0_d]),
            deps=[t for tt, _ in ln1[:4] for t in tt], tag="ag_xin0")
        cc_xin1 = P.cc(lambda g: g.collective_compute(
            "AllGather", ALU.bypass, replica_groups=[list(range(NC))],
            ins=[xin_sl_d[TS // 2:TS, :]], outs=[xin_fu1_d]),
            deps=[t for tt, _ in ln1[4:] for t in tt], tag="ag_xin1")

        def finish(last_deps, dbg_src=None, dbg_bf16=True):
            toks = list(last_deps)
            if dbg_src is not None:
                toks = [P.dma("gp", lambda e: e.dma_start(out=dbg_d, in_=dbg_src),
                              deps=toks)]
                toks = [P.dma("sy", lambda e: e.dma_start(out=out_slab, in_=dbg_d),
                              deps=toks)]
            self_tok = P.ev("sy", lambda e: None, deps=toks, tag="final")
            with nc.Block() as blk:
                P.run(blk)

        if STAGE == "xin":
            finish([cc_xin0, cc_xin1], dbg_src=xin_sl_d)
            return nc

        last_pv_of_batch = None
        ctx_store_toks = []

        with (
            nc.sbuf_tensor("wq_sb", [128, 32 * W3], BF16) as wq_sb,
            nc.sbuf_tensor("s_bqb", [1, W3], BF16) as s_bqb,
            nc.sbuf_tensor("xin_s", [128, 32, 512], BF16) as xin_s,
            nc.sbuf_tensor("qkv_sb", [128, 16, W3], BF16) as qkv_sb,
            nc.sbuf_tensor("sdg", [128, 128], F32) as s_diag,
            nc.sbuf_tensor("ssc", [128, S], F32) as s_sc,
            nc.sbuf_tensor("prb", [128, S], BF16) as probs,
            nc.sbuf_tensor("qk_dm", [128, 2, S], BF16) as qk_dm,
            nc.sbuf_tensor("pT", [128, 8, 128], BF16) as pT_sb,
            nc.sbuf_tensor("rdg", [128, 2, 128], BF16) as rd_sb,
            nc.sbuf_tensor("cxe", [128, 2, 128], BF16) as ctx_sb,
            nc.sbuf_tensor("astt", [128, 2, 8], F32) as stat,
        ):
            e_bq = P.dma("gp", lambda e: e.dma_start(out=s_bqb[:, :], in_=bq_sl),
                         deps=[cc_xin0])
            e_wql = P.dma("sy", lambda e: e.dma_start(
                out=wq_sb[:, :].rearrange("p (c n) -> p c n", n=W3),
                in_=wqb_d.rearrange("(c p) n -> p c n", p=128)),
                deps=[e_wqc, cc_xin0])
            prev_scores = None
            prev_exps = None
            prev_evac = None
            prev_sc_user = None
            prev_ctx_evac = None
            prev_mx0 = None
            prev_rdg_user = None
            st_pv = {"cx": None, "last_ptr": None, "last_pv": None,
                     "rdg_user": None}
            prev_subs = [None] * 4
            e_alh = None
            for b in range(B):
                qkv_evac = []
                prev_half_lastmm = None
                for half in range(4):
                    gt = b * S + half * 512
                    loads = []
                    if half % 2 == 0:
                        xsrc0, ccx = xin_fu0_d, cc_xin0
                        ro = gt // 2
                    else:
                        xsrc0, ccx = xin_fu1_d, cc_xin1
                        ro = (gt - 512) // 2
                    for f in range(32):
                        dep = [ccx, e_wql]
                        if prev_subs[f // 8] is not None:
                            dep.append(prev_subs[f // 8])
                        loads.append(P.dma("sy", (lambda f=f, ro=ro,
                                     xsrc0=xsrc0: lambda e:
                                     e.dma_start(out=xin_s[:, f, :],
                                                 in_=xsrc0[ro:ro + 512,
                                                           f * 128:(f + 1) * 128],
                                                 transpose=True))(), deps=dep))
                    sub_last = [None] * 4
                    for tc4 in range(4):
                        gtc = half * 4 + tc4
                        subs = []
                        for k in range(4):
                            deps = loads[k * 8:(k + 1) * 8]
                            if k == 0:
                                deps = deps + [e_onb]
                                if len(qkv_evac) >= 2:
                                    deps.append(qkv_evac[-2])
                                if last_pv_of_batch is not None:
                                    deps.append(last_pv_of_batch)
                                if prev_ctx_evac is not None:
                                    deps.append(prev_ctx_evac)
                                if prev_evac is not None:
                                    deps.append(prev_evac)
                                if prev_mx0 is not None:
                                    deps.append(prev_mx0)
                            else:
                                deps = deps + [subs[-1]]
                            if k == 3:
                                deps = deps + [e_bq]

                            def emit_qs(e, tc4=tc4, gtc=gtc, k=k):
                                pq = p_a if gtc % 2 == 0 else p_b
                                mm = None
                                for f in range(k * 8, (k + 1) * 8):
                                    for o in range(0, W3, 512):
                                        mm = e.matmul(
                                            pq[:, o:o + 512],
                                            lhsT=xin_s[:, f,
                                                       tc4 * 128:
                                                       (tc4 + 1) * 128],
                                            rhs=wq_sb[:, f * W3 + o:
                                                      f * W3 + o + 512],
                                            start=(f == 0), stop=False)
                                if k == 3:
                                    for o in range(0, W3, 512):
                                        mm = e.matmul(
                                            pq[:, o:o + 512],
                                            lhsT=s_onesb[0:1, 0:128],
                                            rhs=s_bqb[:, o:o + 512],
                                            start=False, stop=True)
                                return mm
                            sub = P.ev("pe", emit_qs, deps=deps)
                            subs.append(sub)
                            sub_last[k] = sub
                        mm = subs[-1]
                        prev_half_lastmm = mm
                        ev = P.ev("dv", (lambda gtc=gtc: lambda e: (
                            e.tensor_scalar(out=qkv_sb[:, gtc, 0:HPC * HD],
                                            in0=(p_a if gtc % 2 == 0 else p_b)[:, 0:HPC * HD],
                                            scalar1=SCALE,
                                            scalar2=None, op0=ALU.mult),
                            e.tensor_copy(out=qkv_sb[:, gtc, HPC * HD:W3],
                                          in_=(p_a if gtc % 2 == 0 else p_b)[:, HPC * HD:W3]),
                        )[-1])(), deps=[mm])
                        qkv_evac.append(ev)
                    prev_subs = list(sub_last)

                for h in range(HPC):
                    aldeps = [prev_scores] if prev_scores is not None else []
                    e_alh = P.dma("sy", (lambda h=h: lambda e: e.dma_start(
                        out=s_alibi[:, :], in_=alibi_r[h, :, :]))(), deps=aldeps)
                    tr_copies = []
                    prev_cp = None
                    for tc in range(16):
                        for qk in range(2):
                            col = qk * HPC * HD + h * HD
                            trdeps = [qkv_evac[tc], e_id]
                            if prev_cp is not None:
                                trdeps.append(prev_cp)
                            tr = P.ev("pe", (lambda tc=tc, col=col: lambda e:
                                      e.transpose(p_c[:, 0, :],
                                                  in_=qkv_sb[:, tc, col:col + HD],
                                                  identity=s_id[:, :]))(),
                                      deps=trdeps)
                            cpdeps = [tr]
                            if tc == 0 and qk == 0 and prev_scores is not None:
                                cpdeps.append(prev_scores)
                            cp = P.ev("dv", (lambda tc=tc, qk=qk: lambda e:
                                      e.tensor_copy(
                                          out=qk_dm[:, qk, tc * 128:(tc + 1) * 128],
                                          in_=p_c[:, 0, :]))(), deps=cpdeps)
                            prev_cp = cp
                            tr_copies.append(cp)
                    def emit_pv(pend):
                        qcp, rdiag_tok = pend
                        vcol = 2 * HPC * HD + h * HD
                        nkc = qcp + 1
                        last = {"tr": None, "cp": None, "pv": None}
                        trs = []
                        cps = []
                        pvs = []
                        PVG = int(os.environ.get("MK_PVG", "2"))
                        for g0 in range(0, nkc, PVG):
                            gn = min(PVG, nkc - g0)
                            for i in range(gn):
                                kc = g0 + i
                                trdeps = [rdiag_tok]
                                if kc >= 8:
                                    trdeps.append(cps[kc - 8])
                                tr = P.ev("pe", (lambda kc=kc: lambda e:
                                          e.transpose(
                                              p_c[:, kc % 8, :],
                                              in_=probs[:, kc * 128:(kc + 1) * 128],
                                              identity=s_id[:, :]))(),
                                          deps=trdeps)
                                trs.append(tr)
                            for i in range(gn):
                                kc = g0 + i
                                cpdeps = [trs[kc]]
                                if kc >= 8:
                                    cpdeps.append(pvs[kc - 8])
                                cp = P.ev("dv", (lambda kc=kc: lambda e:
                                            e.tensor_copy(
                                                out=pT_sb[:, kc % 8, :],
                                                in_=p_c[:, kc % 8, :]))(),
                                            deps=cpdeps)
                                cps.append(cp)
                            for i in range(gn):
                                kc = g0 + i
                                pvdeps = [cps[kc]]
                                if kc == 0 and st_pv["cx"] is not None:
                                    pvdeps.append(st_pv["cx"])
                                pv = P.ev("pe", (lambda kc=kc, qcp=qcp, vcol=vcol:
                                          lambda e: e.matmul(
                                              p_b[:, 0:128],
                                              lhsT=qkv_sb[:, kc, vcol:vcol + HD],
                                              rhs=pT_sb[:, kc % 8, :],
                                              start=(kc == 0),
                                              stop=(kc == qcp)))(),
                                          deps=pvdeps)
                                pvs.append(pv)
                        cxdeps = [pvs[-1]]
                        if len(ctx_store_toks) >= 2:
                            cxdeps.append(ctx_store_toks[-2])
                        cx = P.ev("dv", (lambda qcp=qcp: lambda e: e.tensor_copy(
                            out=ctx_sb[:, qcp % 2, :], in_=p_b[:, 0:128]))(),
                            deps=cxdeps)
                        st_pv["cx"] = cx
                        st_pv["last_ptr"] = trs[-1]
                        st_pv["last_pv"] = pvs[-1]
                        st_pv["rdg_user"] = trs[-1]
                        ctx_store_toks.append(P.dma("gp", (
                            lambda h=h, qcp=qcp, b=b: lambda e: e.dma_start(
                                out=ctxT_d[h * HD:(h + 1) * HD,
                                           b * S + qcp * 128:
                                           b * S + (qcp + 1) * 128],
                                in_=ctx_sb[:, qcp % 2, :]))(), deps=[cx]))

                    pending = None
                    for qc in range(NQC):
                        pfx = (qc + 1) * 128
                        so = qc % 2
                        deps = [tr_copies[-1], e_onb, e_alh]
                        if prev_evac is not None:
                            deps.append(prev_evac)
                        if prev_mx0 is not None:
                            deps.append(prev_mx0)

                        def emit_scores(e, qc=qc, pfx=pfx, h=h):
                            o = 0
                            while o < pfx:
                                nn = min(512, pfx - o)
                                e.matmul(p_a[:, o:o + nn],
                                         lhsT=qk_dm[:, 0, qc * 128:(qc + 1) * 128],
                                         rhs=qk_dm[:, 1, o:o + nn],
                                         start=True, stop=False)
                                o += nn
                            o = 0
                            mm = None
                            while o < pfx:
                                nn = min(512, pfx - o)
                                mm = e.matmul(p_a[:, o:o + nn],
                                              lhsT=s_onesb[0:2, 0:128],
                                              rhs=s_alibi[0:2, o:o + nn],
                                              start=False, stop=True)
                                o += nn
                            return mm
                        sc = P.ev("pe", emit_scores, deps=deps)
                        prev_scores = sc
                        dgdeps = [sc, e_tri]
                        if prev_exps is not None:
                            dgdeps.append(prev_exps)
                        dg = P.ev("dv", (lambda qc=qc: lambda e: e.tensor_tensor(
                            out=s_diag[:, :], in0=p_a[:, qc * 128:(qc + 1) * 128],
                            in1=s_tri[:, :], op=ALU.add))(), deps=dgdeps)
                        if qc > 0:
                            sevdeps = [sc]
                            if prev_sc_user is not None:
                                sevdeps.append(prev_sc_user)
                            sev = P.ev("dv", (lambda qc=qc: lambda e:
                                       e.tensor_copy(out=s_sc[:, 0:qc * 128],
                                                     in_=p_a[:, 0:qc * 128]))(),
                                       deps=sevdeps)
                        else:
                            sev = dg
                        prev_evac = sev
                        if pending is not None:
                            emit_pv(pending)
                            pending = None
                        if qc > 0:
                            mx0 = P.ev("dv", (lambda qc=qc, so=so: lambda e: (
                                e.reduce_max(stat[:, so, 0:1], p_a[:, 0:qc * 128],
                                             axis=AX.X, negate=True),
                                e.reduce_max(stat[:, so, 1:2], s_diag[:, :],
                                             axis=AX.X, negate=True),
                            )[-1])(), deps=[dg, sc])
                            mx = P.ev("dv", (lambda so=so: lambda e:
                                      e.tensor_tensor(out=stat[:, so, 2:3],
                                                      in0=stat[:, so, 0:1],
                                                      in1=stat[:, so, 1:2],
                                                      op=ALU.min))(), deps=[mx0])
                        else:
                            mx0 = P.ev("dv", (lambda so=so: lambda e: e.reduce_max(
                                stat[:, so, 2:3], s_diag[:, :], axis=AX.X,
                                negate=True))(), deps=[dg])
                            mx = mx0
                        prev_mx0 = mx0
                        exdeps = [mx]
                        if st_pv["last_ptr"] is not None:
                            exdeps.append(st_pv["last_ptr"])
                        prev_rdg_user = st_pv["rdg_user"]

                        def emit_exps(e, qc=qc, pfx=pfx, so=so):
                            if qc > 0:
                                e.activation(out=probs[:, 0:qc * 128],
                                             in_=s_sc[:, 0:qc * 128], func=AF.Exp,
                                             bias=stat[:, so, 2:3], scale=1.0,
                                             accum_out=stat[:, so, 3:4])
                            return e.activation(out=probs[:, qc * 128:pfx],
                                                in_=s_diag[:, :], func=AF.Exp,
                                                bias=stat[:, so, 2:3], scale=1.0,
                                                accum_out=stat[:, so, 4:5])
                        ex = P.ev("ac", emit_exps, deps=exdeps)
                        prev_exps = ex
                        prev_sc_user = ex

                        def emit_rsum(e, qc=qc, so=so):
                            if qc > 0:
                                return e.tensor_tensor(out=stat[:, so, 5:6],
                                                       in0=stat[:, so, 3:4],
                                                       in1=stat[:, so, 4:5],
                                                       op=ALU.add)
                            return e.tensor_copy(out=stat[:, so, 5:6],
                                                 in_=stat[:, so, 4:5])
                        rsum = P.ev("dv", emit_rsum, deps=[ex])
                        rcp = P.ev("dv", (lambda so=so: lambda e: e.reciprocal(
                            stat[:, so, 6:7], stat[:, so, 5:6]))(), deps=[rsum])
                        psdeps = [rcp]
                        if prev_rdg_user is not None:
                            psdeps.append(prev_rdg_user)
                        pscale = P.ev("dv", (lambda pfx=pfx, so=so: lambda e:
                                     e.tensor_scalar(out=probs[:, 0:pfx],
                                                     in0=probs[:, 0:pfx],
                                                     scalar1=stat[:, so, 6:7],
                                                     scalar2=None,
                                                     op0=ALU.mult))(),
                                     deps=psdeps)
                        pending = (qc, pscale)
                    emit_pv(pending)
                    pending = None
                    prev_ctx_evac = st_pv["cx"]
                    last_pv_of_batch = st_pv["last_pv"]

        e_w1c = P.dma("gp", lambda e: e.dma_start(out=w1b_d, in_=w1_sl),
                      sem="w1cs")
        e_w2c = P.dma("gp", lambda e: e.dma_start(out=w2b_d, in_=w2_sl),
                      sem="w2cs")

        if STAGE == "attn":
            finish([ctx_store_toks[-1]],
                   dbg_src=ctxT_d.rearrange("a (b c) -> (a b) c", c=D))
            return nc

        att_stores = []
        with (
            nc.sbuf_tensor("wo_sb", [128, HPC, D], BF16) as wo_sb,
            nc.sbuf_tensor("cxt", [128, 2, HPC, 128], BF16) as cxt,
            nc.sbuf_tensor("aop", [128, 2, D], BF16) as aop,
        ):
            wo_lds = [P.dma("gp", (lambda hh=hh: lambda e: e.dma_start(
                out=wo_sb[:, hh, :], in_=wo_sl[hh * 128:(hh + 1) * 128, :]))(),
                deps=[ctx_store_toks[-1]])
                for hh in range(HPC)]
            prev_mm = [None, None]
            prev_ev1 = [None, None]
            P4ORD = [tc for ph in (0, 1) for tc in range(NTC)
                     if (tc % 8) // 4 == ph]
            cc_att0 = None
            for kk, tc in enumerate(P4ORD):
                j = kk % 2
                cld = []
                for hh in range(HPC):
                    dep = [ctx_store_toks[-1]]
                    if prev_mm[j] is not None:
                        dep.append(prev_mm[j])
                    cld.append(P.dma("sy", (lambda tc=tc, hh=hh, j=j: lambda e:
                               e.dma_start(out=cxt[:, j, hh, :],
                                           in_=ctxT_d[hh * 128:(hh + 1) * 128,
                                                      tc * 128:(tc + 1) * 128]))(),
                               deps=dep))
                gevs = []
                mm1 = None
                for g in range(8):
                    po = (g % 4) * 512
                    deps = cld + wo_lds
                    if len(gevs) >= 4:
                        deps.append(gevs[g - 4])
                    elif prev_ev1[0] is not None:
                        deps.append(prev_ev1[0][g % 4])

                    def emit_op(e, g=g, j=j, po=po):
                        mm = None
                        for hh in range(HPC):
                            mm = e.matmul(p_a[:, po:po + 512],
                                          lhsT=cxt[:, j, hh, :],
                                          rhs=wo_sb[:, hh,
                                                    g * 512:(g + 1) * 512],
                                          start=(hh == 0),
                                          stop=(hh == HPC - 1))
                        return mm
                    mm1 = P.ev("pe", emit_op, deps=deps)
                    evdeps = [mm1]
                    if len(att_stores) >= 2:
                        evdeps.append(att_stores[-2])
                    gevs.append(P.ev("dv", (lambda g=g, j=j, po=po: lambda e:
                              e.tensor_copy(
                                  out=aop[:, j, g * 512:(g + 1) * 512],
                                  in_=p_a[:, po:po + 512]))(), deps=evdeps))
                ev0, ev1 = gevs[0], gevs[-1]
                prev_mm[j] = mm1
                prev_ev1[0] = gevs[4:8]
                dst = (att_pt0_d[kk * 128:(kk + 1) * 128, :] if kk < 32 else
                       att_pt1_d[(kk - 32) * 128:(kk - 31) * 128, :])
                att_stores.append(P.dma("sy", (lambda dst=dst, j=j: lambda e:
                                  e.dma_start(out=dst,
                                              in_=aop[:, j, :]))(),
                                  deps=[ev0, ev1], sem="ats0" if kk < 32 else "ats1"))
                if kk == 31:
                    cc_att0 = P.cc(lambda g: g.collective_compute(
                        "ReduceScatter", ALU.add,
                        replica_groups=[list(range(NC))],
                        ins=[att_pt0_d], outs=[att_rs0_d]),
                        deps=list(att_stores), tag="rs_att0")

        if STAGE == "outproj":
            finish([att_stores[-1]], dbg_src=att_pt0_d[0:TS, :])
            return nc

        cc_att1 = P.cc(lambda g: g.collective_compute(
            "ReduceScatter", ALU.add, replica_groups=[list(range(NC))],
            ins=[att_pt1_d], outs=[att_rs1_d]), deps=att_stores[32:],
            tag="rs_att1")

        s_c2t = sb("s_c2t", [RANK, SLICE], BF16)
        s_c1tb = sb("s_c1tb", [128, 32 * RANK], BF16)
        s_iotan = sb("s_iotan", [128, SPB])
        s_blkoff = sb("s_blkoff", [128, BPC])
        h_sb = sb("h_sb", [128, BPC * NTC])
        idx1_sb = sb("idx1_sb", [128, BPC * NTC], I32)
        idx1f_sb = sb("idx1f_sb", [128, BPC * NTC])
        idx2_sb = sb("idx2_sb", [128, BPC * NTC], I32)
        e_c2 = P.dma("gp", lambda e: e.dma_start(out=s_c2t[:, :], in_=c2t_sl),
                     deps=[cc_att0, att_stores[-1]])
        e_c1l = P.dma("gp", lambda e: e.dma_start(
            out=s_c1tb[:, :].rearrange("p (c r) -> p c r", r=RANK),
            in_=c1tb_d.rearrange("(c p) r -> p c r", p=128)),
            deps=[e_c1c, cc_att0, att_stores[-1]])
        e_io1 = P.ev("gp", lambda e: e.iota(
            s_iotan[:, :], pattern=[[-1, SPB]], base=SPB - 1, channel_multiplier=0,
            allow_small_or_imprecise_dtypes=True),
            deps=[cc_att0, att_stores[-1]])
        e_io2 = P.ev("gp", lambda e: e.iota(
            s_blkoff[:, :], pattern=[[SPB, BPC]], base=0, channel_multiplier=0,
            allow_small_or_imprecise_dtypes=True),
            deps=[cc_att0, att_stores[-1]])

        if STAGE == "attsum":
            d1 = P.dma("gp", lambda e: e.dma_start(
                out=dbg_d[0:TS // 2, :], in_=att_rs0_d), deps=[cc_att0])
            d2 = P.dma("gp", lambda e: e.dma_start(
                out=dbg_d[TS // 2:TS, :], in_=att_rs1_d), deps=[cc_att1])
            finish([d1, d2], dbg_src=dbg_d)
            return nc

        with (
            nc.sbuf_tensor("res_a", [128, 2, D], BF16) as res_a,
            nc.sbuf_tensor("s_bo", [128, D], F32) as s_bo,
            nc.sbuf_tensor("xf_fm", [128, 32, TS], BF16) as xf_fm,
            nc.sbuf_tensor("r_sb", [RANK, TS], BF16) as r_sb,
            ExitStack() as lst2,
        ):
            p4_guard = att_stores[-1]
            e_bo = P.dma("sy", lambda e: e.dma_start(out=s_bo[:, :], in_=bout_bc),
                         deps=[cc_att0, p4_guard])
            tr_last = {}
            all_tr = []
            xf_stores = []

            def ld2(i, dst, war):
                j = i % 2
                lx = P.dma("sy", (lambda i=i, dst=dst: lambda e: e.dma_start(
                    out=dst, in_=x_slab[i * 128:(i + 1) * 128, :]))(),
                    deps=war + [p4_guard])
                rs_src = (att_rs0_d[i * 128:(i + 1) * 128, :] if i < 4 else
                          att_rs1_d[(i - 4) * 128:(i - 3) * 128, :])
                la = P.dma("sy", (lambda src=rs_src, j=j: lambda e:
                    e.dma_start(out=res_a[:, j, :], in_=src))(),
                    deps=[cc_att0 if i < 4 else cc_att1, p4_guard] + war)
                sm0 = P.ev("dv", (lambda j=j, dst=dst: lambda e:
                           e.tensor_tensor(out=dst, in0=dst,
                                           in1=res_a[:, j, :], op=ALU.add))(),
                           deps=[lx, la])
                sm = P.ev("dv", (lambda j=j, dst=dst: lambda e:
                          e.tensor_tensor(out=dst, in0=dst,
                                          in1=s_bo[:, :], op=ALU.add))(),
                          deps=[sm0, e_bo])
                P.dma("sy", (lambda i=i, dst=dst: lambda e: e.dma_start(
                    out=xp_d[i * 128:(i + 1) * 128, :], in_=dst))(), deps=[sm])
                return sm

            def st2(i, src, deps):
                toks = [P.dma("gp", (lambda i=i, src=src: lambda e: e.dma_start(
                    out=xf_sl_d[i * 128:(i + 1) * 128, :], in_=src))(),
                    deps=deps, sem="xfs0" if i < 4 else "xfs1")]
                xf_stores.append(toks[0])
                for ch in range(32):
                    trdeps = list(deps) + [e_id]
                    if "t" in tr_last:
                        trdeps.append(tr_last["t"])
                    tr = P.ev("pe", (lambda ch=ch, src=src: lambda e: e.transpose(
                        p_c[:, 0, :], in_=src[:, ch * 128:(ch + 1) * 128],
                        identity=s_id[:, :]))(), deps=trdeps)
                    cp = P.ev("dv", (lambda ch=ch, i=i: lambda e: e.tensor_copy(
                        out=xf_fm[:, ch, i * 128:(i + 1) * 128],
                        in_=p_c[:, 0, :]))(), deps=[tr])
                    tr_last["t"] = cp
                    all_tr.append(cp)
                toks.append(tr_last["t"])
                return toks

            ln2 = emit_ln(ld2, st2, NT_SL, "p6", lst2)

            def emit_rproj(ph):
                def f(e, ph=ph):
                    mm = None
                    for ch in range(32):
                        mm = e.matmul(p_b[0:RANK, ph * 512:ph * 512 + 512],
                                      lhsT=s_c1tb[:, ch * RANK:(ch + 1) * RANK],
                                      rhs=xf_fm[:, ch,
                                                ph * 512:ph * 512 + 512],
                                      start=(ch == 0), stop=(ch == 31))
                    return mm
                return f
            rmm0 = P.ev("pe", emit_rproj(0),
                        deps=[all_tr[32 * 4 - 1], e_c1l, prev_ctx_evac])
            rev0 = P.ev("dv", lambda e: e.tensor_copy(
                out=r_sb[:, 0:512], in_=p_b[0:RANK, 0:512]), deps=[rmm0])
            rstr0 = P.dma("gp", lambda e: e.dma_start(
                out=r_sl0_d, in_=r_sb[:, 0:512]), deps=[rev0], sem="rss")
            rmm1 = P.ev("pe", emit_rproj(1), deps=[all_tr[-1], e_c1l])
            rev1 = P.ev("dv", lambda e: e.tensor_copy(
                out=r_sb[:, 512:1024], in_=p_b[0:RANK, 512:1024]),
                deps=[rmm1])
            rstr1 = P.dma("gp", lambda e: e.dma_start(
                out=r_sl1_d, in_=r_sb[:, 512:1024]), deps=[rev1], sem="rss")

        cc_xf0 = P.cc(lambda g: g.collective_compute(
            "AllGather", ALU.bypass, replica_groups=[list(range(NC))],
            ins=[xf_sl_d[0:TS // 2, :]], outs=[xf_fu0_d]),
            deps=xf_stores[:4], tag="ag_xf0")
        cc_r0 = P.cc(lambda g: g.collective_compute(
            "AllGather", ALU.bypass, replica_groups=[list(range(NC))],
            ins=[r_sl0_d], outs=[r_st0_d]), deps=[rstr0], tag="ag_r0")
        cc_xf1 = P.cc(lambda g: g.collective_compute(
            "AllGather", ALU.bypass, replica_groups=[list(range(NC))],
            ins=[xf_sl_d[TS // 2:TS, :]], outs=[xf_fu1_d]),
            deps=xf_stores[4:], tag="ag_xf1")
        cc_r1 = P.cc(lambda g: g.collective_compute(
            "AllGather", ALU.bypass, replica_groups=[list(range(NC))],
            ins=[r_sl1_d], outs=[r_st1_d]), deps=[rstr1], tag="ag_r1")

        if STAGE == "xf":
            finish([cc_xf0, cc_xf1, cc_r0, cc_r1], dbg_src=xf_sl_d)
            return nc

        with (
            nc.sbuf_tensor("r_t", [RANK, 4, 128], BF16) as r_t,
            nc.sbuf_tensor("amx", [128, BPC], F32) as amx,
            nc.sbuf_tensor("amall", [128, SLICE], BF16) as amall,
            nc.sbuf_tensor("amt", [128, BPC * 4], F32) as amt,
            nc.sbuf_tensor("xf_t", [128, 2, D], BF16) as xf_t,
            nc.sbuf_tensor("g1", [128, 8, D], BF16) as g1,
            nc.sbuf_tensor("pmul", [128, 2, D], BF16) as pmul,
            nc.sbuf_tensor("hacc", [128, 2, BPC], F32) as hacc,
            nc.sbuf_tensor("g2", [128, BPC, D], BF16) as g2,
            nc.sbuf_tensor("hd", [128, 2 * BPC, 128], BF16) as hd,
            nc.sbuf_tensor("fo", [128, D], BF16) as fo,
        ):
            W2G = [(o, 512) for o in range(0, D, 512)]
            TORD = [tt for ph in (0, 1) for tt in range(NTC)
                    if (tt % 8) // 4 == ph]
            am_toks = [None] * (NTC + 1)
            a2_list = [None] * (NTC + 1)
            imm_list = [None] * (NTC + 2)
            rl_list = [None] * (NTC + 4)
            xl_list = [None] * (NTC + 2)
            h_toks = [None] * NTC
            mul_list = []
            red_list = []
            gt1s = {}
            g2_list = [None] * NTC
            hd_list = [None] * NTC
            wmm_list = [None] * NTC
            ev_list = [None] * NTC
            st_list = [None] * NTC
            ffn_stores = []

            def emit_rl(t):
                if t >= NTC:
                    return
                tl = TORD[t]
                dep = [cc_r1 if t < 6 else (cc_r0 if t < 32 else cc_r1)]
                if t >= 4 and imm_list[t - 4] is not None:
                    dep.append(imm_list[t - 4])
                rsrc = (r_st0_d[RANK * (tl // 8):RANK * (tl // 8) + RANK,
                                128 * (tl % 8):128 * (tl % 8) + 128]
                        if t < 32 else
                        r_st1_d[RANK * (tl // 8):RANK * (tl // 8) + RANK,
                                128 * (tl % 8 - 4):128 * (tl % 8 - 4) + 128])
                rl_list[t] = P.dma("sy", (lambda t=t, rsrc=rsrc: lambda e:
                    e.dma_start(out=r_t[:, t % 4, :], in_=rsrc))(),
                    deps=dep, sem="rls")

            def emit_imm(t):
                if t >= NTC:
                    return
                deps = [rl_list[t], e_c2]
                if t >= 1:
                    deps.append(a2_list[t - 1])

                def f(e, j=t % 4):
                    mm = None
                    for o in range(0, SLICE, 512):
                        nn = min(512, SLICE - o)
                        mm = e.matmul(p_a[:, o:o + nn], lhsT=r_t[:, j, :],
                                      rhs=s_c2t[:, o:o + nn], start=True,
                                      stop=True)
                    return mm
                imm_list[t] = P.ev("pe", f, deps=deps)

            def emit_xl(t):
                if t >= NTC:
                    return
                tl = TORD[t]
                dep = [cc_r1 if t < 4 else (cc_xf0 if t < 32 else cc_xf1)]
                if t >= 2:
                    dep.append(mul_list[(t - 1) * BPC - 1])
                ro = ((tl // 8) * 512 + (tl % 8) * 128 if t < 32 else
                      (tl // 8) * 512 + (tl % 8 - 4) * 128)
                xsrc = (xf_fu0_d[ro:ro + 128, :] if t < 32 else
                        xf_fu1_d[ro:ro + 128, :])
                xl_list[t] = P.dma("sy", (lambda t=t, xsrc=xsrc: lambda e:
                    e.dma_start(out=xf_t[:, t % 2, :], in_=xsrc))(), deps=dep,
                    sem="xls")

            def emit_argmax(t):
                if t >= NTC:
                    return
                a1 = P.ev("dv", lambda e: e.reduce_max(
                    amx[:, :], p_a[:, 0:SLICE].rearrange("p (b s) -> p b s",
                                                         s=SPB), axis=AX.X),
                    deps=[imm_list[t]])

                def emit_eq(e):
                    mm = None
                    for bb in range(BPC):
                        mm = e.scalar_tensor_tensor(
                            out=amall[:, bb * SPB:(bb + 1) * SPB],
                            in0=p_a[:, bb * SPB:(bb + 1) * SPB],
                            scalar=amx[:, bb:bb + 1], in1=s_iotan[:, :],
                            op0=ALU.is_equal, op1=ALU.mult)
                    return mm
                a2 = P.ev("dv", emit_eq, deps=[a1, e_io1])
                a2_list[t] = a2
                a3 = P.ev("dv", lambda e: e.reduce_max(
                    amt[:, 0:BPC],
                    amall[:, :].rearrange("p (b s) -> p b s", s=SPB),
                    axis=AX.X), deps=[a2])
                a4 = P.ev("dv", lambda e: e.tensor_scalar(
                    out=amt[:, BPC:2 * BPC], in0=amt[:, 0:BPC],
                    scalar1=-1.0, scalar2=float(SPB - 1),
                    op0=ALU.mult, op1=ALU.add), deps=[a3])
                a5 = P.ev("dv", lambda e: e.tensor_tensor(
                    out=amt[:, 2 * BPC:3 * BPC], in0=amt[:, BPC:2 * BPC],
                    in1=s_blkoff[:, :], op=ALU.add), deps=[a4, e_io2])
                a6 = P.ev("dv", (lambda t=t: lambda e: e.tensor_scalar(
                    out=idx1f_sb[:, t * BPC:(t + 1) * BPC],
                    in0=amt[:, 2 * BPC:3 * BPC], scalar1=float(D),
                    scalar2=None, op0=ALU.mult))(), deps=[a5])
                am_toks[t] = P.ev("dv", (lambda t=t: lambda e: e.tensor_copy(
                    out=idx1_sb[:, t * BPC:(t + 1) * BPC],
                    in_=idx1f_sb[:, t * BPC:(t + 1) * BPC]))(), deps=[a6])

            def emit_g1(t, bb, war):
                if t >= NTC:
                    return
                pi = t * BPC + bb
                agd = [cc_r1] if t < 4 else []
                gt1s[(t, bb)] = P.dma("gp", (lambda t=t, bb=bb, j2=pi % 8:
                           lambda e: e.indirect_dma_start(
                               out=g1[:, j2, :], out_offset=None,
                               in_=w1b_d.rearrange("(x p) d -> x (p d)",
                                                   x=128),
                               in_offset=IndirectOffsetOnAxis(
                                   idx1_sb[:, t * BPC + bb:t * BPC + bb + 1],
                                   axis=1),
                               bounds_check=None))(),
                           deps=[am_toks[t], e_w1c] + war + agd, sem="g1s")

            def emit_g2bb(t, bb):
                if t < 0:
                    return
                gdeps2 = [am_toks[t], e_w2c]
                if t < 4:
                    gdeps2.append(cc_r1)
                if t >= 1 and wmm_list[t - 1] is not None:
                    gdeps2.append(wmm_list[t - 1])
                if g2_list[t] is None:
                    g2_list[t] = []
                g2_list[t].append(P.dma("gp", (lambda t=t, bb=bb: lambda e:
                           e.indirect_dma_start(
                               out=g2[:, bb, :],
                               out_offset=None,
                               in_=w2b_d.rearrange("(x p) d -> x (p d)",
                                                   x=128),
                               in_offset=IndirectOffsetOnAxis(
                                   idx1_sb[:, t * BPC + bb:
                                           t * BPC + bb + 1],
                                   axis=1),
                               bounds_check=None))(), deps=gdeps2,
                           sem="g2s"))

            def emit_hd(t):
                hds = []
                for bb in range(BPC):
                    hdd = [h_toks[t], e_id]
                    if t >= 2 and wmm_list[t - 2] is not None:
                        hdd.append(wmm_list[t - 2])
                    hds.append(P.ev("dv", (lambda t1=t, bb=bb: lambda e:
                               e.tensor_scalar(
                                   out=hd[:, (t1 % 2) * BPC + bb, :],
                                   in0=s_id[:, :],
                                   scalar1=h_sb[:, t1 * BPC + bb:
                                                t1 * BPC + bb + 1],
                                   scalar2=None, op0=ALU.mult))(),
                               deps=hdd))
                hd_list[t] = hds

            def emit_w2grp(t, gi, evs):
                go, gw = W2G[gi]
                po = (gi % 2) * 512
                deps2 = g2_list[t] + hd_list[t]
                if gi >= 2:
                    deps2.append(evs[gi - 2])
                elif t >= 1 and ev_list[t - 1] is not None:
                    deps2.append(ev_list[t - 1][6 + gi])

                def f(e, go=go, gw=gw, t=t, po=po):
                    mm = None
                    for bb in range(BPC):
                        s2 = (t % 2) * BPC + bb
                        mm = e.matmul(p_b[:, po:po + gw],
                                      lhsT=hd[:, s2, :],
                                      rhs=g2[:, bb, go:go + gw],
                                      start=(bb == 0),
                                      stop=(bb == BPC - 1))
                    return mm
                wmm = P.ev("pe", f, deps=deps2)
                evdeps = [wmm]
                if t >= 1 and st_list[t - 1] is not None:
                    evdeps.append(st_list[t - 1])
                gev = P.ev("dv", (lambda go=go, gw=gw, po=po:
                           lambda e: e.tensor_copy(
                               out=fo[:, go:go + gw],
                               in_=p_b[:, po:po + gw]))(), deps=evdeps)
                evs.append(gev)
                return wmm

            for k in range(3):
                emit_rl(k)
            emit_imm(0)
            emit_xl(0)
            emit_xl(1)
            emit_argmax(0)
            emit_rl(3)
            emit_imm(1)
            for k in range(BPC):
                emit_g1(0, k, [])

            cc_ffn0_box = [None]

            def emit_w2tail(t1):
                evs = []
                wmm = None
                for gi in range(8):
                    wmm = emit_w2grp(t1, gi, evs)
                wmm_list[t1] = wmm
                ev_list[t1] = evs
                sdst = (ffn_pt0_d[t1 * 128:(t1 + 1) * 128, :] if t1 < 32
                        else ffn_pt1_d[(t1 - 32) * 128:(t1 - 31) * 128, :])
                st_list[t1] = P.dma("sy", (lambda sdst=sdst: lambda e:
                                 e.dma_start(out=sdst,
                                             in_=fo[:, :]))(), deps=evs,
                                 sem="sts0" if t1 < 32 else "sts1")
                ffn_stores.append(st_list[t1])

            for t in range(NTC):
                if t >= 1:
                    emit_hd(t - 1)
                emit_argmax(t + 1)
                emit_rl(t + 4)
                emit_imm(t + 2)
                reds = []
                for bb in range(BPC):
                    pi = t * BPC + bb
                    mldeps = [gt1s[(t, bb)], xl_list[t]]
                    if pi >= 2:
                        mldeps.append(red_list[pi - 2])
                    ml = P.ev("dv", (lambda j2=pi % 8, jp=pi % 2, j=t % 2:
                              lambda e:
                              e.tensor_mul(pmul[:, jp, :], g1[:, j2, :],
                                           xf_t[:, j, :]))(), deps=mldeps)
                    mul_list.append(ml)
                    rddeps = [ml]
                    if t >= 2:
                        rddeps.append(h_toks[t - 2])
                    rd = P.ev("ac", (lambda bb=bb, j2=pi % 2, hj=t % 2:
                              lambda e: e.activation(
                                  out=pmul[:, j2, :], in_=pmul[:, j2, :],
                                  func=AF.Copy,
                                  accum_out=hacc[:, hj, bb:bb + 1]))(),
                              deps=rddeps)
                    red_list.append(rd)
                    reds.append(rd)
                    if t >= 1:
                        emit_g2bb(t - 1, bb)
                hf = P.ev("dv", (lambda t=t, hj=t % 2: lambda e:
                    e.tensor_scalar(
                        out=h_sb[:, t * BPC:(t + 1) * BPC],
                        in0=hacc[:, hj, :], scalar1=0.0, scalar2=None,
                        op0=ALU.max))(), deps=reds)
                h_toks[t] = hf
                if t >= 1:
                    emit_w2tail(t - 1)
                emit_xl(t + 2)
                for k in range(BPC):
                    pw = t * BPC + k - 1
                    emit_g1(t + 1, k, [mul_list[pw]] if pw >= 0 else [])


            t = NTC
            emit_hd(NTC - 1)
            for bb in range(BPC):
                emit_g2bb(NTC - 1, bb)
            cc_ffn0_box[0] = P.cc(lambda g: g.collective_compute(
                "ReduceScatter", ALU.add,
                replica_groups=[list(range(NC))],
                ins=[ffn_pt0_d], outs=[ffn_rs0_d]),
                deps=st_list[0:32], tag="rs_ffn0")
            emit_w2tail(NTC - 1)
            am_toks = [x for x in am_toks if x is not None]
            h_toks = [x for x in h_toks if x is not None]

        if STAGE == "ffnp":
            finish([ffn_stores[-1]], dbg_src=ffn_pt0_d[0:TS, :])
            return nc
        if STAGE == "am":
            t1 = P.dma("gp", lambda e: e.dma_start(
                out=dbg_d[0:128, 0:BPC * NTC], in_=idx1f_sb[:, :]),
                deps=am_toks + h_toks)
            t2 = P.dma("gp", lambda e: e.dma_start(
                out=dbg_d[128:256, 0:BPC * NTC], in_=h_sb[:, :]),
                deps=am_toks + h_toks)
            t3 = P.dma("sy", lambda e: e.dma_start(out=out_slab, in_=dbg_d),
                       deps=[t1, t2])
            finish([t3])
            return nc

        cc_ffn0 = cc_ffn0_box[0]
        cc_ffn1 = P.cc(lambda g: g.collective_compute(
            "ReduceScatter", ALU.add, replica_groups=[list(range(NC))],
            ins=[ffn_pt1_d], outs=[ffn_rs1_d]), deps=ffn_stores[32:],
            tag="rs_ffn1")

        with (
            nc.sbuf_tensor("f_x", [128, 2, D], F32) as f_x,
            nc.sbuf_tensor("f_f", [128, 2, D], BF16) as f_f,
            nc.sbuf_tensor("f_o", [128, 2, D], F32) as f_o,
        ):
            fin_toks = []
            prev_add = [None, None]
            for i in range(NT_SL):
                j = i % 2
                war = [prev_add[j]] if prev_add[j] else []
                ccd = cc_ffn0 if i < 4 else cc_ffn1
                fsrc = (ffn_rs0_d[i * 128:(i + 1) * 128, :] if i < 4 else
                        ffn_rs1_d[(i - 4) * 128:(i - 3) * 128, :])
                lx = P.dma("sy", (lambda i=i, j=j: lambda e: e.dma_start(
                    out=f_x[:, j, :],
                    in_=xp_d[i * 128:(i + 1) * 128, :]))(), deps=war + [ccd])
                lf = P.dma("sy", (lambda fsrc=fsrc, j=j: lambda e:
                    e.dma_start(out=f_f[:, j, :], in_=fsrc))(),
                    deps=[ccd] + war)
                ad = P.ev("dv", (lambda j=j: lambda e: e.tensor_tensor(
                    out=f_o[:, j, :], in0=f_x[:, j, :], in1=f_f[:, j, :],
                    op=ALU.add))(), deps=[lx, lf])
                prev_add[j] = ad
                fin_toks.append(P.dma("sy", (lambda i=i, j=j: lambda e:
                                e.dma_start(
                                    out=out_slab[i * 128:(i + 1) * 128, :],
                                    in_=f_o[:, j, :]))(), deps=[ad]))

        finish(fin_toks)
        return nc

    return nc


_NC_CACHE = {}


def kernel(L0_attn_hidden_states, L0_attn_w_qkv, L0_attn_b_qkv, L0_attn_w_out,
           L0_attn_b_out, L0_ffn_c1_w, L0_ffn_c2_w, L0_ffn_w1, L0_ffn_w2t):
    x = np.ascontiguousarray(np.asarray(L0_attn_hidden_states,
                                        dtype=np.float32)).reshape(T, D)
    wq = np.asarray(L0_attn_w_qkv, dtype=np.float32)
    bq = np.asarray(L0_attn_b_qkv, dtype=np.float32)
    wo = np.asarray(L0_attn_w_out, dtype=np.float32)
    bo = np.asarray(L0_attn_b_out, dtype=np.float32)
    c1 = np.asarray(L0_ffn_c1_w, dtype=np.float32)
    c2 = np.asarray(L0_ffn_c2_w, dtype=np.float32)
    w1 = np.asarray(L0_ffn_w1, dtype=np.float32)
    w2 = np.asarray(L0_ffn_w2t, dtype=np.float32)

    slopes = get_alibi_slopes(H)
    pos = np.arange(S, dtype=np.float32)
    alibi = slopes[:, None] * pos[None, :]
    al_hi = alibi.astype(ml_dtypes.bfloat16)
    al_lo = (alibi - al_hi.astype(np.float32)).astype(ml_dtypes.bfloat16)
    al_hl = np.stack([al_hi, al_lo], axis=1)
    tri = np.where(np.arange(128)[None, :] > np.arange(128)[:, None],
                   np.float32(-1.0e9), np.float32(0.0)).astype(np.float32)
    ident = np.eye(128, dtype=ml_dtypes.bfloat16)
    onesb = np.ones((2, 128), dtype=ml_dtypes.bfloat16)
    bout = np.ascontiguousarray(np.broadcast_to(bo[None, :], (128, D)))
    c1t_host = np.ascontiguousarray(c1.T)

    in_maps = []
    for c in range(NC):
        hs = slice(c * HPC * HD, (c + 1) * HPC * HD)
        wq_c = np.ascontiguousarray(
            np.concatenate([wq[:, 0 * D:][:, hs], wq[:, 1 * D:][:, hs],
                            wq[:, 2 * D:][:, hs]], axis=1))
        bq_c = np.ascontiguousarray(
            np.concatenate([bq[0 * D:1 * D][hs], bq[1 * D:2 * D][hs],
                            bq[2 * D:3 * D][hs]])[None, :])
        rs = slice(c * SLICE, (c + 1) * SLICE)
        al_c = np.ascontiguousarray(al_hl[c * HPC:(c + 1) * HPC])
        in_maps.append({
            "x_slab": np.ascontiguousarray(x[c * TS:(c + 1) * TS]),
            "wq_sl": wq_c, "bq_sl": bq_c,
            "wo_sl": np.ascontiguousarray(wo[c * HPC * HD:(c + 1) * HPC * HD]),
            "bout_bc": bout,
            "c1t": c1t_host,
            "c2t_sl": np.ascontiguousarray(c2[rs].T),
            "w1_sl": np.ascontiguousarray(w1[rs]),
            "w2_sl": np.ascontiguousarray(w2[rs]),
            "alibi_r": al_c, "tri_c": tri, "ident_c": ident,
            "onesb_c": onesb,
        })

    key = STAGE
    if key not in _NC_CACHE:
        _NC_CACHE[key] = build()
    res = run_bass_kernel_spmd(_NC_CACHE[key], in_maps, core_ids=list(range(NC)))
    out = np.concatenate([np.asarray(r["out_slab"]) for r in res.results],
                         axis=0)
    return out.reshape(B, S, D).astype(np.float32)


if __name__ == "__main__":
    nc = build()
    print("build ok; instructions:",
          {k: len(v) for k, v in
           {e: [1] * 0 for e in ENGS}.items()} if False else "n/a")
